# version 08
# speedup vs baseline: 1.0988x; 1.0362x over previous
import contextlib
import numpy as np
import ml_dtypes
import concourse.bass as bass
import concourse.mybir as mybir
from concourse.bass_utils import run_bass_kernel_spmd

F32 = mybir.dt.float32
BF16 = mybir.dt.bfloat16
I32 = mybir.dt.int32
ALU = mybir.AluOpType
AF = mybir.ActivationFunctionType
AX = mybir.AxisListType.X

L = 4096
LH = L // 2
H = 1024
CH = 512
CC = 128
NCH = L // CC
MAGIC = float(np.float32(1.5 * 2.0 ** 23))
QMAX = 32767.0

DEV_STOP = None
DEV_TAPS = []
DEV_STAGES = None
S6_LEVEL = 4


def _bf(x):
    return np.asarray(x, dtype=ml_dtypes.bfloat16)


class B:

    def __init__(self, nc, es):
        self.nc = nc
        self.es = es
        self.stack = es
        self._nsem = 0

    def sb(self, name, shape, dt=BF16):
        return self.stack.enter_context(self.nc.sbuf_tensor(name, shape, dt))

    def sem(self, name):
        self._nsem += 1
        return self.es.enter_context(self.nc.semaphore(f"{name}{self._nsem}"))


def _barrier(bx, block, sm_bar, nbar):
    nbar[0] += 1
    tgt = 5 * nbar[0]
    for eng in ("sync", "tensor", "vector", "scalar", "gpsimd"):
        def mk(e):
            e.drain()
            e.nop().then_inc(sm_bar, 1)
            e.wait_ge(sm_bar, tgt)
        getattr(block, eng)(mk)


def build_program():
    nc = bass.Bass("TRN2", target_bir_lowering=False)

    def ein(name, shape, dt=F32):
        return nc.dram_tensor(name, shape, dt, kind="ExternalInput").ap()

    g = {}
    g["x_in"] = ein("x", [L, H])
    g["xmy_in"] = ein("x_my", [LH, H])
    g["tok0_in"] = ein("tok0", [1, 1], I32)
    g["wq_in"] = ein("wq", [H, CH])
    g["wk_in"] = ein("wk", [H, CH])
    g["wv_in"] = ein("wv", [H, CH])
    g["wb_in"] = ein("wb", [H, 2])
    g["cqw_in"] = ein("cqw", [CH, 4])
    g["ckw_in"] = ein("ckw", [CH, 4])
    g["cvw_in"] = ein("cvw", [CH, 4])
    g["fsw_in"] = ein("fsw", [CH, 3])
    g["flw_in"] = ein("flw", [CH, 64])
    g["w1_in"] = ein("w1", [1024, 2048])
    g["w1s_in"] = ein("w1s", [64, 2048])
    g["b1_in"] = ein("b1", [1, 2048])
    g["w2_in"] = ein("w2", [2048, 16])
    g["b2_in"] = ein("b2", [1, 16])
    g["glt_in"] = ein("glt", [1, 4])
    g["onw_in"] = ein("onw", [1, 256])
    g["onwc_in"] = ein("onwc", [256, 1])
    g["wsel_in"] = ein("wsel", [16, 8])
    g["mA_in"] = ein("mA", [128, 1])
    g["mB_in"] = ein("mB", [128, 1])
    g["wo_in"] = ein("wo", [CH, H])
    g["wu_my_in"] = ein("wu_my", [3072, 128])
    g["wg_my_in"] = ein("wg_my", [3072, 128])
    g["wd_my_in"] = ein("wd_my", [H, 384])
    g["out"] = nc.dram_tensor("out", [LH, H], F32, kind="ExternalOutput").ap()

    def scr(name, shape, dt=BF16):
        ap = nc.dram_tensor(name, shape, dt, kind="Internal").ap()
        g[name] = ap
        return ap

    scr("ln1tok_d", [L, H])
    scr("ln1d_d", [H, L])
    scr("qpost_d", [CH, L])
    scr("kpost_d", [CH, L])
    scr("vpost_d", [CH, L])
    scr("beta_d", [L, 2], F32)
    scr("facq_d", [1, L], F32)
    scr("fack_d", [1, L], F32)
    scr("fackb_d", [1, L], F32)
    scr("knt_d", [L, CH])
    scr("kbnt_d", [L, CH])
    scr("vpt_d", [L, CH])
    scr("qnd_d", [CH, L])
    scr("knd_d", [CH, L])
    scr("kbnd_d", [CH, L])
    scr("firs_d", [CH, L])
    scr("firl_d", [CH, L])
    scr("delta_d", [CH, L])
    scr("stats_my", [32, 2048], F32)
    scr("stats_all", [64, 2048], F32)
    scr("ln1my_d", [LH, H])
    scr("wgt_my", [LH, 16], F32)
    scr("wgt_all", [L, 16], F32)
    scr("o_my_d", [CH, L])
    scr("o_all_d", [H, L])
    scr("wugT_my", [128, 6144])
    scr("wdT_my", [384, H])
    scr("wugT_all", [H, 6144])
    scr("wdT_all", [3072, H])
    scr("mxi0", [128, 1], F32)
    scr("mxo0", [128, 1], F32)
    scr("mxi1", [128, 2], F32)
    scr("mxo1", [128, 2], F32)
    scr("mxi3", [128, 1], F32)
    scr("mxo3", [128, 1], F32)
    scr("mxi4", [128, 1], F32)
    scr("mxo4", [128, 1], F32)
    scr("attnp_d", [L, H])
    scr("attnsum_d", [L, H])
    scr("x2_d", [LH, H], F32)
    scr("x3_d", [LH, H], F32)
    scr("upraw_d", [LH, 3072], F32)
    scr("gateraw_d", [LH, 3072], F32)
    scr("g2raw_d", [LH, 3072], F32)
    scr("oraw_d", [LH, 3072], F32)
    scr("dbg_d", [128, 16384])

    g["ident_c"] = nc.inline_tensor(np.eye(128, dtype=ml_dtypes.bfloat16), "identc").ap()
    g["identf_c"] = nc.inline_tensor(np.eye(128, dtype=np.float32), "identfc").ap()
    g["ones_c"] = nc.inline_tensor(np.ones((1, 128), dtype=ml_dtypes.bfloat16), "onesc").ap()
    g["onesf_c"] = nc.inline_tensor(np.ones((128, 128), dtype=ml_dtypes.bfloat16), "onesfc").ap()
    g["stril_c"] = nc.inline_tensor(_bf(np.tril(np.ones((128, 128)), -1)), "strilc").ap()
    g["striu_c"] = nc.inline_tensor(_bf(np.triu(np.ones((128, 128)), 1)), "striuc").ap()
    g["triui_c"] = nc.inline_tensor(_bf(np.triu(np.ones((128, 128)))), "triuic").ap()
    sel = np.zeros((16, 128, 64), np.float32)
    for w_ in range(2):
        for bh in range(8):
            sel[w_ * 8 + bh, :, w_ * 32 + bh] = 1.0 / 256.0
    g["sel_c"] = nc.inline_tensor(_bf(sel.transpose(1, 0, 2).reshape(128, 16 * 64)), "selc").ap()

    es = contextlib.ExitStack()
    with es:
        bx = B(nc, es)
        bx.g = g
        sm_bar = bx.sem("bar")
        g["ccg"] = bx.sem("ccg")
        nbar = [0]

        g["ident_s"] = bx.sb("ident_s", [128, 128])
        g["identf_s"] = bx.sb("identf_s", [128, 128], F32)
        g["ones_s"] = bx.sb("ones_s", [1, 128])
        g["stril_s"] = bx.sb("stril_s", [128, 128])
        g["striu_s"] = bx.sb("striu_s", [128, 128])
        g["triui_s"] = bx.sb("triui_s", [128, 128])
        g["tok0_s"] = bx.sb("tok0_s", [1, 1], I32)
        g["onesf_s"] = bx.sb("onesf_s", [128, 128])
        g["facbuf"] = bx.sb("facbuf", [128, 3 * 64], F32)
        bx.PS = [es.enter_context(nc.psum_tensor(f"psb{i}", [128, 512], F32))
                 for i in range(8)]

        stages = DEV_STAGES or [_s0_ln1, _s1_qkv, _s2_fir, _s3_fac, _s4_delta,
                                _s5_stats, _s6a_lnmy, _s6_gate, _s7_combine,
                                _s8_wo, _s9a_dq, _s9b_ln3, _s9e_upg, _s9f_sig,
                                _s9g_oraw, _s9_down]
        for st in stages:
            with contextlib.ExitStack() as stage_es:
                bx.stack = stage_es
                with nc.Block(st.__name__) as block:
                    st(bx, block)
                    _barrier(bx, block, sm_bar, nbar)
            bx.stack = es
            if DEV_STOP == st.__name__[1:].split("_")[0]:
                break

        if DEV_TAPS:
            with nc.Block("taps") as block:
                tsem = bx.sem("tap")
                outs = []
                for i, (nm, shape, dts) in enumerate(DEV_TAPS):
                    dt = {"f32": F32, "bf16": BF16}[dts]
                    top = nc.dram_tensor(f"tap_{nm}", shape, dt, kind="ExternalOutput").ap()
                    outs.append((top, g[nm]))

                @block.gpsimd
                def _(e):
                    for i, (top, src) in enumerate(outs):
                        e.dma_start(out=top, in_=src).then_inc(tsem, 16)
                    e.wait_ge(tsem, 16 * len(outs))
        else:
            pass

    return nc


def _emit_ln(bx, block, x_ap, out_ap, NT, extra_consts=0):
    nc, g = bx.nc, bx.g
    GRP = 8
    NG = NT // GRP
    tag = f"ln{NT}"
    xrow = bx.sb(tag + "_x", [128, GRP * H], F32)
    cols = bx.sb(tag + "_cols", [128, NT * 2], F32)
    dcol = bx.sb(tag + "_dcol", [128, NT * 3], F32)
    ln1t = bx.sb(tag + "_out", [128, GRP * H])
    ld = bx.sem("ld")
    sv = bx.sem("v")
    ss = bx.sem("s")
    sg = bx.sem("g")
    vf = bx.sem("vf")
    bx._lnsg = sg

    @block.sync
    def _(e):
        if extra_consts:
            _load_consts(bx, e, ld)
        for gi in range(NG):
            if gi >= 1:
                e.wait_ge(sv, gi * (GRP + 1))
            for j in range(GRP):
                i = gi * GRP + j
                e.dma_start(out=xrow[:, j * H:(j + 1) * H],
                            in_=x_ap[i * 128:(i + 1) * 128, :]).then_inc(ld, 16)

    @block.scalar
    def _(e):
        for gi in range(NG):
            e.wait_ge(ld, 16 * extra_consts + 16 * 8 * (gi + 1))
            if gi >= 1:
                e.wait_ge(sg, 16 * 8 * gi)
            for j in range(GRP):
                i = gi * GRP + j
                e.activation(ln1t[:, j * H:(j + 1) * H], xrow[:, j * H:(j + 1) * H],
                             AF.Square,
                             accum_out=cols[:, 2 * i:2 * i + 1]).then_inc(ss, 1)
            e.wait_ge(sv, gi * (GRP + 1) + 1)
            e.activation(dcol[:, NT + gi * GRP:NT + (gi + 1) * GRP],
                         dcol[:, NT + gi * GRP:NT + (gi + 1) * GRP],
                         AF.Sqrt).then_inc(ss, 1)

    @block.vector
    def _(e):
        for gi in range(NG):
            for j in range(GRP):
                i = gi * GRP + j
                e.wait_ge(ss, gi * (GRP + 1) + j + 1)
                e.reduce_sum(cols[:, 2 * i + 1:2 * i + 2],
                             xrow[:, j * H:(j + 1) * H], axis=AX).then_inc(vf, 1)
            base = gi * 12
            e.wait_ge(vf, base + 8)
            i0, i1 = gi * GRP, (gi + 1) * GRP
            sum_ap = cols[:, :].rearrange("p (t k) -> p t k", k=2)[:, i0:i1, 1]
            sq_ap = cols[:, :].rearrange("p (t k) -> p t k", k=2)[:, i0:i1, 0]
            e.tensor_scalar(dcol[:, i0:i1], sum_ap, -1.0 / H, None, ALU.mult).then_inc(vf, 1)
            e.wait_ge(vf, base + 9)
            e.tensor_tensor(dcol[:, NT + i0:NT + i1], dcol[:, i0:i1], dcol[:, i0:i1], ALU.mult).then_inc(vf, 1)
            e.wait_ge(vf, base + 10)
            e.tensor_scalar(dcol[:, NT + i0:NT + i1], dcol[:, NT + i0:NT + i1], -1.0, 1e-5, ALU.mult, ALU.add).then_inc(vf, 1)
            e.wait_ge(vf, base + 11)
            e.scalar_tensor_tensor(dcol[:, NT + i0:NT + i1], sq_ap, 1.0 / H,
                                   dcol[:, NT + i0:NT + i1], ALU.mult, ALU.add).then_inc(sv, 1)
            e.wait_ge(ss, (gi + 1) * (GRP + 1))
            e.reciprocal(dcol[:, 2 * NT + i0:2 * NT + i1], dcol[:, NT + i0:NT + i1]).then_inc(vf, 1)
            e.wait_ge(vf, base + 12)
            if gi >= 1:
                e.wait_ge(sg, 16 * 8 * gi)
            for j in range(GRP):
                i = gi * GRP + j
                e.tensor_scalar(ln1t[:, j * H:(j + 1) * H], xrow[:, j * H:(j + 1) * H],
                                dcol[:, i:i + 1], dcol[:, 2 * NT + i:2 * NT + i + 1],
                                ALU.add, ALU.mult).then_inc(sv, 1)

    @block.gpsimd
    def _(e):
        for gi in range(NG):
            for j in range(GRP):
                i = gi * GRP + j
                e.wait_ge(sv, gi * (GRP + 1) + j + 2)
                e.dma_start(out=out_ap[i * 128:(i + 1) * 128, :],
                            in_=ln1t[:, j * H:(j + 1) * H]).then_inc(sg, 16)
        e.wait_ge(sg, 16 * NT)


def _load_consts(bx, e, ld):
    g = bx.g
    e.dma_start(out=g["ident_s"][:, :], in_=g["ident_c"]).then_inc(ld, 16)
    e.dma_start(out=g["identf_s"][:, :], in_=g["identf_c"]).then_inc(ld, 16)
    e.dma_start(out=g["ones_s"][:, :], in_=g["ones_c"]).then_inc(ld, 16)
    e.dma_start(out=g["stril_s"][:, :], in_=g["stril_c"]).then_inc(ld, 16)
    e.dma_start(out=g["striu_s"][:, :], in_=g["striu_c"]).then_inc(ld, 16)
    e.dma_start(out=g["triui_s"][:, :], in_=g["triui_c"]).then_inc(ld, 16)
    e.dma_start(out=g["tok0_s"][:, :], in_=g["tok0_in"]).then_inc(ld, 16)
    e.dma_start(out=g["onesf_s"][:, :], in_=g["onesf_c"]).then_inc(ld, 16)


def _s0_ln1(bx, block):
    _emit_ln(bx, block, bx.g["x_in"], bx.g["ln1tok_d"], 32, extra_consts=8)


def _s1_qkv(bx, block):
    nc, g = bx.nc, bx.g
    NT8 = 8
    wqkv = bx.sb("s1_w", [128, 3 * 32 * 128])
    wbs = bx.sb("s1_wb", [128, 8 * 2])
    convw = bx.sb("s1_cw", [128, 12 * 4], F32)
    diag = bx.sb("s1_diag", [128, 12 * 4 * 128])
    ln1d = bx.sb("s1_ln1d", [128, 2 * 8 * 512])
    rawwin = bx.sb("s1_rawwin", [128, 12 * 576])
    post = bx.sb("s1_post", [128, 12 * 512])
    brow = bx.sb("s1_brow", [2, L], F32)
    btok = bx.sb("s1_btok", [128, 4], F32)
    pj = bx.PS[0][:, :]
    pc = bx.PS[1][:, :]
    pb = bx.PS[2][:, :]
    pbt = bx.PS[3][:, :]
    ld0 = bx.sem("ld0")
    ld1 = bx.sem("ld1")
    lds = (ld0, ld1)
    lw = bx.sem("lw")
    tt = bx.sem("t")
    vv = bx.sem("v")
    ssc = bx.sem("s")
    sg = bx.sem("g")
    TPT = 25

    @block.gpsimd
    def _(e):
        for t, src_ in enumerate((g["wq_in"], g["wk_in"], g["wv_in"])):
            for kk in range(8):
                for mm in range(4):
                    off = (t * 32 + kk * 4 + mm) * 128
                    e.dma_start(out=wqkv[:, off:off + 128],
                                in_=src_[kk * 128:(kk + 1) * 128, mm * 128:(mm + 1) * 128]
                                ).then_inc(lw, 16)
        for kk in range(8):
            e.dma_start(out=wbs[:, 2 * kk:2 * kk + 2],
                        in_=g["wb_in"][kk * 128:(kk + 1) * 128, :]).then_inc(lw, 16)
        for t, src_ in enumerate((g["cqw_in"], g["ckw_in"], g["cvw_in"])):
            for ch in range(4):
                ci = t * 4 + ch
                e.dma_start(out=convw[:, ci * 4:ci * 4 + 4],
                            in_=src_[ch * 128:(ch + 1) * 128, :]).then_inc(lw, 16)
        dsts = (g["qpost_d"], g["kpost_d"], g["vpost_d"])
        for n in range(NT8):
            for c in range(12):
                e.wait_ge(ssc, n * 13 + c + 1)
                t, ch = c // 4, c % 4
                e.dma_start(out=dsts[t][ch * 128:(ch + 1) * 128, n * 512:(n + 1) * 512],
                            in_=post[:, c * 512:(c + 1) * 512]).then_inc(sg, 16)
            for c in range(8):
                e.wait_ge(lds[n % 2], 16 * 8 * (n // 2 + 1))
                e.dma_start(out=g["ln1d_d"][c * 128:(c + 1) * 128, n * 512:(n + 1) * 512],
                            in_=ln1d[:, ((n % 2) * 8 + c) * 512:((n % 2) * 8 + c) * 512 + 512]
                            ).then_inc(sg, 16)
        for i in range(32):
            e.wait_ge(vv, 1 + NT8 * 24 + i + 1)
            e.wait_ge(sg, 16 * (20 * NT8 + i))
            e.dma_start(out=g["beta_d"][i * 128:(i + 1) * 128, :],
                        in_=btok[:, 2 * (i % 2):2 * (i % 2) + 2]).then_inc(sg, 16)
        e.wait_ge(sg, 16 * (20 * NT8 + 32))

    @block.vector
    def _(e):
        e.wait_ge(lw, 16 * 116)
        for c in range(12):
            for tp in range(4):
                off = (c * 4 + tp) * 128
                e.tensor_scalar(diag[:, off:off + 128], g["ident_s"][:, :],
                                convw[:, c * 4 + tp:c * 4 + tp + 1], None, ALU.mult)
        e.memset(rawwin[:, :], 0.0).then_inc(vv, 1)
        for n in range(NT8):
            for c in range(12):
                e.wait_ge(tt, n * TPT + 2 * c + 1)
                e.tensor_copy(rawwin[:, c * 576 + 64:c * 576 + 576], pj).then_inc(vv, 1)
                e.wait_ge(tt, n * TPT + 2 * c + 2)
                e.tensor_copy(rawwin[:, c * 576:c * 576 + 64],
                              rawwin[:, c * 576 + 512:c * 576 + 576]).then_inc(vv, 1)
        for i in range(32):
            e.wait_ge(tt, NT8 * TPT + i + 1)
            if i >= 2:
                e.wait_ge(sg, 16 * (20 * NT8 + i - 1))
            e.tensor_copy(btok[:, 2 * (i % 2):2 * (i % 2) + 2], pbt[:, :2]).then_inc(vv, 1)

    @block.tensor
    def _(e):
        e.wait_ge(vv, 1)
        for n in range(NT8):
            e.wait_ge(lds[n % 2], 16 * 8 * (n // 2 + 1))
            for c in range(12):
                for kk in range(8):
                    woff = ((c // 4) * 32 + kk * 4 + (c % 4)) * 128
                    mi = e.matmul(pj, wqkv[:, woff:woff + 128],
                                  ln1d[:, ((n % 2) * 8 + kk) * 512:((n % 2) * 8 + kk) * 512 + 512],
                                  start=(kk == 0), stop=(kk == 7))
                mi.then_inc(tt, 1)
                e.wait_ge(vv, 1 + n * 24 + 2 * c + 1)
                if c >= 1:
                    e.wait_ge(ssc, n * 13 + c)
                elif n >= 1:
                    e.wait_ge(ssc, (n - 1) * 13 + 12)
                for tp in range(4):
                    mi = e.matmul(pc, diag[:, (c * 4 + tp) * 128:(c * 4 + tp) * 128 + 128],
                                  rawwin[:, c * 576 + 61 + tp:c * 576 + 61 + tp + 512],
                                  start=(tp == 0), stop=(tp == 3))
                mi.then_inc(tt, 1)
            if n >= 1:
                e.wait_ge(ssc, (n - 1) * 13 + 13)
            for kk in range(8):
                mi = e.matmul(pb[:2, :], wbs[:, 2 * kk:2 * kk + 2],
                              ln1d[:, ((n % 2) * 8 + kk) * 512:((n % 2) * 8 + kk) * 512 + 512],
                              start=(kk == 0), stop=(kk == 7))
            mi.then_inc(tt, 1)
        e.wait_ge(ssc, NT8 * 13)
        for i in range(32):
            if i >= 1:
                e.wait_ge(vv, 1 + 24 * NT8 + i)
            e.transpose(pbt[:, :2], brow[:2, i * 128:(i + 1) * 128],
                        g["identf_s"][:2, :2]).then_inc(tt, 1)

    @block.scalar
    def _(e):
        for n in range(NT8):
            for c in range(12):
                e.wait_ge(tt, n * TPT + 2 * c + 2)
                if n >= 1 and c == 0:
                    e.wait_ge(sg, 16 * 20 * n)
                e.activation(post[:, c * 512:(c + 1) * 512], pc,
                             AF.Silu).then_inc(ssc, 1)
            e.wait_ge(tt, n * TPT + TPT)
            e.activation(brow[:2, n * 512:(n + 1) * 512], pb[:2, :],
                         AF.Sigmoid).then_inc(ssc, 1)

    @block.sync
    def _(e):
        for n in range(NT8):
            if n >= 2:
                e.wait_ge(tt, (n - 1) * TPT + TPT)
                e.wait_ge(sg, 16 * ((n - 2) * 20 + 20))
            for c in range(8):
                e.dma_start(
                    out=ln1d[:, ((n % 2) * 8 + c) * 512:((n % 2) * 8 + c) * 512 + 512],
                    in_=g["ln1tok_d"][n * 512:(n + 1) * 512, c * 128:(c + 1) * 128],
                    transpose=True).then_inc(lds[n % 2], 16)


def _s2_fir(bx, block):
    nc, g = bx.nc, bx.g
    vwin = bx.sb("s2_vwin", [128, 64 + L])
    wcols = bx.sb("s2_wc", [128, 3 + 64], F32)
    diag_l = bx.sb("s2_dl", [128, 63 * 128])
    diag_s = bx.sb("s2_ds", [128, 3 * 128])
    post_l = bx.sb("s2_pl", [128, 2 * 512])
    post_s = bx.sb("s2_ps", [128, 2 * 512])
    PL = (bx.PS[0], bx.PS[1])
    PSs = (bx.PS[2], bx.PS[3])
    lv = bx.sem("lv")
    tt = bx.sem("t")
    vv = bx.sem("v")
    vs = bx.sem("vs")
    ss = bx.sem("s")
    sg = bx.sem("g")

    @block.gpsimd
    def _(e):
        for cc in range(4):
            if cc >= 1:
                e.wait_ge(tt, 16 * cc)
                e.wait_ge(lv, 48 * cc)
            e.dma_start(out=vwin[:, 64:], in_=g["vpost_d"][cc * 128:(cc + 1) * 128, :]
                        ).then_inc(lv, 16)
            e.dma_start(out=wcols[:, 0:3], in_=g["fsw_in"][cc * 128:(cc + 1) * 128, :]
                        ).then_inc(lv, 16)
            e.dma_start(out=wcols[:, 3:67], in_=g["flw_in"][cc * 128:(cc + 1) * 128, :]
                        ).then_inc(lv, 16)
            for n in range(8):
                idx = cc * 8 + n
                e.wait_ge(ss, idx + 1)
                e.dma_start(out=g["firl_d"][cc * 128:(cc + 1) * 128, n * 512:(n + 1) * 512],
                            in_=post_l[:, (n % 2) * 512:(n % 2) * 512 + 512]).then_inc(sg, 16)
                e.wait_ge(vs, idx + 1)
                e.dma_start(out=g["firs_d"][cc * 128:(cc + 1) * 128, n * 512:(n + 1) * 512],
                            in_=post_s[:, (n % 2) * 512:(n % 2) * 512 + 512]).then_inc(sg, 16)
        e.wait_ge(sg, 16 * 64)

    @block.vector
    def _(e):
        e.memset(vwin[:, 0:64], 0.0).then_inc(vv, 1)
        for cc in range(4):
            e.wait_ge(lv, 48 * (cc + 1))
            if cc >= 1:
                e.wait_ge(tt, 16 * cc)
            for j in range(63):
                e.tensor_scalar(diag_l[:, j * 128:(j + 1) * 128], g["ident_s"][:, :],
                                wcols[:, 3 + j:4 + j], None, ALU.mult).then_inc(vv, 1)
            for j in range(3):
                e.tensor_scalar(diag_s[:, j * 128:(j + 1) * 128], g["ident_s"][:, :],
                                wcols[:, j:j + 1], None, ALU.mult).then_inc(vv, 1)
            for n in range(8):
                idx = cc * 8 + n
                e.wait_ge(tt, cc * 16 + 2 * n + 2)
                if idx >= 1:
                    e.wait_ge(sg, 32 * idx)
                e.tensor_copy(post_s[:, (n % 2) * 512:(n % 2) * 512 + 512],
                              PSs[n % 2][:, :]).then_inc(vs, 1)

    @block.scalar
    def _(e):
        for cc in range(4):
            for n in range(8):
                idx = cc * 8 + n
                e.wait_ge(tt, cc * 16 + 2 * n + 1)
                if idx >= 1:
                    e.wait_ge(sg, 32 * idx)
                e.activation(post_l[:, (n % 2) * 512:(n % 2) * 512 + 512],
                             PL[n % 2][:, :], AF.Identity).then_inc(ss, 1)

    @block.tensor
    def _(e):
        for cc in range(4):
            e.wait_ge(vv, 1 + 66 * (cc + 1))
            e.wait_ge(lv, 48 * (cc + 1))
            for n in range(8):
                idx = cc * 8 + n
                if idx >= 2:
                    e.wait_ge(ss, idx - 1)
                for j in range(63):
                    mi = e.matmul(PL[n % 2][:, :],
                                  diag_l[:, j * 128:(j + 1) * 128],
                                  vwin[:, n * 512 + 2 + j:n * 512 + 2 + j + 512],
                                  start=(j == 0), stop=(j == 62))
                mi.then_inc(tt, 1)
                if idx >= 2:
                    e.wait_ge(vs, idx - 1)
                for j in range(3):
                    mi = e.matmul(PSs[n % 2][:, :],
                                  diag_s[:, j * 128:(j + 1) * 128],
                                  vwin[:, n * 512 + 62 + j:n * 512 + 62 + j + 512],
                                  start=(j == 0), stop=(j == 2))
                mi.then_inc(tt, 1)


def _s3_fac(bx, block):
    nc, g = bx.nc, bx.g
    facbuf = g["facbuf"]
    qt = bx.sb("s3_qt", [128, 2 * 512])
    kt = bx.sb("s3_kt", [128, 2 * 512])
    vt = bx.sb("s3_vt", [128, 2 * 512])
    bt = bx.sb("s3_bt", [128, 2 * 2], F32)
    sqs = bx.sb("s3_sqs", [128, 2 * 1024])
    cbuf = bx.sb("s3_cb", [128, 2 * 16], F32)
    okn = bx.sb("s3_okn", [128, 2 * 512])
    okbn = bx.sb("s3_okbn", [128, 2 * 512])
    ovp = bx.sb("s3_ovp", [128, 2 * 512])
    la0 = bx.sem("la0")
    la1 = bx.sem("la1")
    las = (la0, la1)
    sa = bx.sem("sa")
    va = bx.sem("va")
    ga0 = bx.sem("ga0")
    ga1 = bx.sem("ga1")
    gas = (ga0, ga1)

    @block.sync
    def _(e):
        for i in range(32):
            if i >= 2:
                e.wait_ge(va, (i - 1) * 12)
            p = i % 2
            e.dma_start(out=qt[:, p * 512:(p + 1) * 512],
                        in_=g["qpost_d"][:, i * 128:(i + 1) * 128], transpose=True
                        ).then_inc(las[p], 16)
            e.dma_start(out=kt[:, p * 512:(p + 1) * 512],
                        in_=g["kpost_d"][:, i * 128:(i + 1) * 128], transpose=True
                        ).then_inc(las[p], 16)
            e.dma_start(out=vt[:, p * 512:(p + 1) * 512],
                        in_=g["vpost_d"][:, i * 128:(i + 1) * 128], transpose=True
                        ).then_inc(las[p], 16)
            e.dma_start(out=bt[:, p * 2:p * 2 + 2],
                        in_=g["beta_d"][i * 128:(i + 1) * 128, :]).then_inc(las[p], 16)

    @block.scalar
    def _(e):
        for i in range(32):
            p = i % 2
            e.wait_ge(las[p], 64 * (i // 2 + 1))
            for h in range(2):
                e.activation(sqs[:, (p * 4 + h) * 256:(p * 4 + h) * 256 + 256],
                             kt[:, p * 512 + h * 256:p * 512 + h * 256 + 256],
                             AF.Square,
                             accum_out=cbuf[:, p * 16 + h:p * 16 + h + 1]).then_inc(sa, 1)
                e.activation(sqs[:, (p * 4 + 2 + h) * 256:(p * 4 + 2 + h) * 256 + 256],
                             qt[:, p * 512 + h * 256:p * 512 + h * 256 + 256],
                             AF.Square,
                             accum_out=cbuf[:, p * 16 + 2 + h:p * 16 + 2 + h + 1]).then_inc(sa, 1)
            e.wait_ge(va, i * 12 + 1)
            e.activation(cbuf[:, p * 16 + 4:p * 16 + 8],
                         cbuf[:, p * 16 + 4:p * 16 + 8], AF.Sqrt).then_inc(sa, 1)

    @block.vector
    def _(e):
        for i in range(32):
            p = i % 2
            e.wait_ge(sa, i * 5 + 4)
            e.tensor_scalar(cbuf[:, p * 16 + 4:p * 16 + 8],
                            cbuf[:, p * 16:p * 16 + 4], 1e-6, None, ALU.add).then_inc(va, 1)
            e.wait_ge(sa, i * 5 + 5)
            e.reciprocal(cbuf[:, p * 16 + 8:p * 16 + 12],
                         cbuf[:, p * 16 + 4:p * 16 + 8]).then_inc(va, 1)
            e.wait_ge(va, i * 12 + 2)
            e.tensor_tensor(cbuf[:, p * 16 + 12:p * 16 + 14],
                            cbuf[:, p * 16 + 8:p * 16 + 10],
                            bt[:, p * 2:p * 2 + 2], ALU.mult).then_inc(va, 1)
            e.wait_ge(va, i * 12 + 3)
            if i >= 2:
                e.wait_ge(gas[i % 2], 48 * (i // 2))
            for h in range(2):
                e.tensor_scalar(okn[:, p * 512 + h * 256:p * 512 + h * 256 + 256],
                                kt[:, p * 512 + h * 256:p * 512 + h * 256 + 256],
                                cbuf[:, p * 16 + 8 + h:p * 16 + 9 + h], None,
                                ALU.mult).then_inc(va, 1)
                e.tensor_scalar(okbn[:, p * 512 + h * 256:p * 512 + h * 256 + 256],
                                kt[:, p * 512 + h * 256:p * 512 + h * 256 + 256],
                                cbuf[:, p * 16 + 12 + h:p * 16 + 13 + h], None,
                                ALU.mult).then_inc(va, 1)
                e.tensor_scalar(ovp[:, p * 512 + h * 256:p * 512 + h * 256 + 256],
                                vt[:, p * 512 + h * 256:p * 512 + h * 256 + 256],
                                bt[:, p * 2 + h:p * 2 + h + 1], None,
                                ALU.mult).then_inc(va, 1)
            e.tensor_copy(facbuf[:, 0 * 64 + 2 * i:0 * 64 + 2 * i + 2],
                          cbuf[:, p * 16 + 8:p * 16 + 10]).then_inc(va, 1)
            e.tensor_copy(facbuf[:, 1 * 64 + 2 * i:1 * 64 + 2 * i + 2],
                          cbuf[:, p * 16 + 12:p * 16 + 14]).then_inc(va, 1)
            e.tensor_copy(facbuf[:, 2 * 64 + 2 * i:2 * 64 + 2 * i + 2],
                          cbuf[:, p * 16 + 10:p * 16 + 12]).then_inc(va, 1)

    @block.gpsimd
    def _(e):
        for i in range(32):
            p = i % 2
            e.wait_ge(va, i * 12 + 9)
            e.dma_start(out=g["knt_d"][i * 128:(i + 1) * 128, :],
                        in_=okn[:, p * 512:(p + 1) * 512]).then_inc(gas[p], 16)
            e.dma_start(out=g["kbnt_d"][i * 128:(i + 1) * 128, :],
                        in_=okbn[:, p * 512:(p + 1) * 512]).then_inc(gas[p], 16)
            e.dma_start(out=g["vpt_d"][i * 128:(i + 1) * 128, :],
                        in_=ovp[:, p * 512:(p + 1) * 512]).then_inc(gas[p], 16)
        e.wait_ge(ga0, 48 * 16)
        e.wait_ge(ga1, 48 * 16)

    qd = bx.sb("s3_qd", [128, 2 * 4 * 512])
    kd = bx.sb("s3_kd", [128, 2 * 4 * 512])
    dg = bx.sb("s3_dg", [128, 6 * 4 * 128])
    oqn = bx.sb("s3_oqn", [128, 2 * 512])
    oknd = bx.sb("s3_oknd", [128, 2 * 512])
    okbnd = bx.sb("s3_okbnd", [128, 2 * 512])
    lb0 = bx.sem("lb0")
    lb1 = bx.sem("lb1")
    lbs = (lb0, lb1)
    tb = bx.sem("tb")
    vb = bx.sem("vb")
    gb = bx.sem("gb")

    @block.sync
    def _(e):
        for n in range(8):
            p = n % 2
            if n >= 2:
                e.wait_ge(vb, (n - 1) * 36)
            for cc in range(4):
                e.dma_start(out=qd[:, (p * 4 + cc) * 512:(p * 4 + cc) * 512 + 512],
                            in_=g["qpost_d"][cc * 128:(cc + 1) * 128, n * 512:(n + 1) * 512]
                            ).then_inc(lbs[p], 16)
                e.dma_start(out=kd[:, (p * 4 + cc) * 512:(p * 4 + cc) * 512 + 512],
                            in_=g["kpost_d"][cc * 128:(cc + 1) * 128, n * 512:(n + 1) * 512]
                            ).then_inc(lbs[p], 16)

    @block.vector
    def _(e):
        e.wait_ge(ga0, 48 * 16)
        e.wait_ge(ga1, 48 * 16)
        for n in range(8):
            p = n % 2
            if n >= 1:
                e.wait_ge(tb, 6 * n)
            for f in range(3):
                for h in range(2):
                    for sub in range(4):
                        col = f * 64 + 2 * (4 * n + sub) + h
                        e.tensor_scalar(dg[:, ((f * 2 + h) * 4 + sub) * 128:
                                            ((f * 2 + h) * 4 + sub) * 128 + 128],
                                        g["ident_s"][:, :], facbuf[:, col:col + 1],
                                        None, ALU.mult).then_inc(vb, 1)
            e.wait_ge(lbs[p], 16 * 8 * (n // 2 + 1))
            e.wait_ge(tb, 6 * (n + 1))
            if n >= 2:
                e.wait_ge(gb, 16 * 3 * 4 * (n - 1))
            for cc in range(4):
                h = cc // 2
                e.tensor_tensor(oqn[:, p * 512:(p + 1) * 512],
                                qd[:, (p * 4 + cc) * 512:(p * 4 + cc) * 512 + 512],
                                bx.PS[4 + h][:, :], ALU.mult).then_inc(vb, 1)
                e.tensor_tensor(oknd[:, p * 512:(p + 1) * 512],
                                kd[:, (p * 4 + cc) * 512:(p * 4 + cc) * 512 + 512],
                                bx.PS[0 + h][:, :], ALU.mult).then_inc(vb, 1)
                e.tensor_tensor(okbnd[:, p * 512:(p + 1) * 512],
                                kd[:, (p * 4 + cc) * 512:(p * 4 + cc) * 512 + 512],
                                bx.PS[2 + h][:, :], ALU.mult).then_inc(vb, 1)
                e.wait_ge(gb, 16 * (12 * n + 3 * cc + 3))

    @block.tensor
    def _(e):
        for n in range(8):
            e.wait_ge(vb, n * 36 + 24)
            for f in range(3):
                for h in range(2):
                    for sub in range(4):
                        mi = e.matmul(bx.PS[f * 2 + h][:, sub * 128:sub * 128 + 128],
                                      g["onesf_s"][:, :],
                                      dg[:, ((f * 2 + h) * 4 + sub) * 128:
                                          ((f * 2 + h) * 4 + sub) * 128 + 128],
                                      start=True, stop=True)
                    mi.then_inc(tb, 1)

    @block.gpsimd
    def _(e):
        for n in range(8):
            p = n % 2
            for cc in range(4):
                e.wait_ge(vb, n * 36 + 24 + 3 * cc + 3)
                e.dma_start(out=g["qnd_d"][cc * 128:(cc + 1) * 128, n * 512:(n + 1) * 512],
                            in_=oqn[:, p * 512:(p + 1) * 512]).then_inc(gb, 16)
                e.dma_start(out=g["knd_d"][cc * 128:(cc + 1) * 128, n * 512:(n + 1) * 512],
                            in_=oknd[:, p * 512:(p + 1) * 512]).then_inc(gb, 16)
                e.dma_start(out=g["kbnd_d"][cc * 128:(cc + 1) * 128, n * 512:(n + 1) * 512],
                            in_=okbnd[:, p * 512:(p + 1) * 512]).then_inc(gb, 16)
        e.wait_ge(gb, 16 * 96)


def _s4_delta(bx, block):
    nc, g = bx.nc, bx.g
    PS = bx.PS
    qn = bx.sb("s4_qn", [128, 2 * 256])
    kn = bx.sb("s4_kn", [128, 2 * 256])
    kbn = bx.sb("s4_kbn", [128, 2 * 256])
    knt = bx.sb("s4_knt", [128, 2 * 256])
    kbnt = bx.sb("s4_kbnt", [128, 2 * 256])
    vpt = bx.sb("s4_vpt", [128, 2 * 256])
    A = bx.sb("s4_A", [128, 7 * 128])
    At = bx.sb("s4_At", [128, 6 * 128])
    attnT = bx.sb("s4_attnT", [128, 128])
    Tt = bx.sb("s4_Tt", [128, 2 * 128])
    u_sb = bx.sb("s4_u", [128, 256])
    wT_sb = bx.sb("s4_wT", [128, 256])
    u2_sb = bx.sb("s4_u2", [128, 256])
    S_sb = bx.sb("s4_S", [128, 2 * 256])
    S_f32 = bx.sb("s4_Sf", [128, 2 * 256], F32)
    ot = bx.sb("s4_ot", [128, 2 * 256])
    l0 = bx.sem("l0")
    l1 = bx.sem("l1")
    ls = (l0, l1)
    tt = bx.sem("t")
    vv = bx.sem("v")
    ss = bx.sem("s")
    sg0 = bx.sem("g0")
    sg1 = bx.sem("g1")
    sgs = (sg0, sg1)
    TPC = 25
    VPC = 24
    SPC = 5

    def vb(ci):
        return ci * VPC + 4 * (ci // 32 + 1)

    def vtot(ci):
        return vb(ci) + VPC

    @block.sync
    def _(e):
        for ci in range(64):
            bh, i = ci // 32, ci % 32
            p = ci % 2
            if ci >= 2:
                e.wait_ge(tt, (ci - 1) * TPC)
            for buf, srcd in ((qn, g["qnd_d"]), (kn, g["knd_d"]), (kbn, g["kbnd_d"])):
                e.dma_start(
                    out=buf[:, :].rearrange("q (p c t) -> q p c t", p=2, c=2)[:, p],
                    in_=srcd[bh * 256:(bh + 1) * 256, i * 128:(i + 1) * 128]
                    .rearrange("(c q) t -> q c t", c=2)).then_inc(ls[p], 16)
            for buf, srcd in ((knt, g["knt_d"]), (kbnt, g["kbnt_d"]), (vpt, g["vpt_d"])):
                e.dma_start(out=buf[:, p * 256:(p + 1) * 256],
                            in_=srcd[i * 128:(i + 1) * 128, bh * 256:(bh + 1) * 256]
                            ).then_inc(ls[p], 16)

    @block.tensor
    def _(e):
        for ci in range(64):
            bh, i = ci // 32, ci % 32
            p = ci % 2
            bt = ci * TPC
            bv = vb(ci)
            e.wait_ge(ls[p], 96 * (ci // 2 + 1))
            if ci >= 1:
                e.wait_ge(vv, vtot(ci - 1))
                e.wait_ge(ss, ci * SPC)
            qnp = qn[:, p * 256:(p + 1) * 256]
            knp = kn[:, p * 256:(p + 1) * 256]
            kbnp = kbn[:, p * 256:(p + 1) * 256]
            kntp = knt[:, p * 256:(p + 1) * 256]
            kbntp = kbnt[:, p * 256:(p + 1) * 256]
            vptp = vpt[:, p * 256:(p + 1) * 256]
            for bank, a_, b_ in ((2, kbnp, knp), (3, knp, kbnp), (4, knp, qnp)):
                for kc in range(2):
                    mi = e.matmul(PS[bank][:, 0:128],
                                  a_[:, kc * 128:kc * 128 + 128],
                                  b_[:, kc * 128:kc * 128 + 128],
                                  start=(kc == 0), stop=(kc == 1))
                mi.then_inc(tt, 1)
            for li in range(5):
                e.wait_ge(vv, bv + 2 if li == 0 else bv + 3 + 2 * li)
                e.matmul(PS[2][:, 0:128], At[:, li * 128:(li + 1) * 128],
                         A[:, li * 128:(li + 1) * 128],
                         start=True, stop=True).then_inc(tt, 1)
                e.matmul(PS[3][:, 0:128], A[:, li * 128:(li + 1) * 128],
                         At[:, li * 128:(li + 1) * 128],
                         start=True, stop=True).then_inc(tt, 1)
            e.wait_ge(vv, bv + 13)
            e.matmul(PS[2][:, 0:128], At[:, 5 * 128:6 * 128],
                     A[:, 5 * 128:6 * 128], start=True, stop=True).then_inc(tt, 1)
            if ci >= 1:
                e.wait_ge(ss, (ci - 1) * SPC + 1)
            for j in range(6):
                e.wait_ge(vv, bv + 15 + j)
                e.matmul(PS[4][:, 0:128], A[:, (j + 1) * 128:(j + 2) * 128],
                         Tt[:, (j % 2) * 128:(j % 2) * 128 + 128],
                         start=True, stop=True).then_inc(tt, 1)
            e.wait_ge(vv, bv + 21)
            e.matmul(PS[5][:, 128:384], Tt[:, 0:128], vptp,
                     start=True, stop=True).then_inc(tt, 1)
            if ci >= 1:
                e.wait_ge(ss, (ci - 1) * SPC + 2)
            for m in range(2):
                mi = e.matmul(PS[6][:, m * 128:m * 128 + 128],
                              kbntp[:, m * 128:m * 128 + 128], Tt[:, 0:128],
                              start=True, stop=True)
            mi.then_inc(tt, 1)
            e.wait_ge(ss, ci * SPC + 2)
            if i == 0:
                e.wait_ge(vv, bv)
            for kc in range(2):
                mi = e.matmul(PS[7][:, 0:256], wT_sb[:, kc * 128:kc * 128 + 128],
                              S_sb[:, kc * 256:kc * 256 + 256],
                              start=(kc == 0), stop=(kc == 1))
            mi.then_inc(tt, 1)
            e.wait_ge(vv, bv + 22)
            for dvh in range(2):
                for kc in range(2):
                    e.matmul(PS[7][:, 256 + dvh * 128:256 + dvh * 128 + 128],
                             S_sb[:, kc * 256 + dvh * 128:kc * 256 + dvh * 128 + 128],
                             qnp[:, kc * 128:kc * 128 + 128],
                             start=(kc == 0), stop=False)
                mi = e.matmul(PS[7][:, 256 + dvh * 128:256 + dvh * 128 + 128],
                              u2_sb[:, dvh * 128:dvh * 128 + 128],
                              attnT[:, :], start=False, stop=True)
            mi.then_inc(tt, 1)
            for kc in range(2):
                mi = e.matmul(PS[kc][:, 0:256], kntp[:, kc * 128:kc * 128 + 128],
                              u2_sb[:, :], start=True, stop=True)
            mi.then_inc(tt, 1)

    @block.vector
    def _(e):
        for ci in range(64):
            bh, i = ci // 32, ci % 32
            bt = ci * TPC
            bv = vb(ci)
            if ci >= 1:
                e.wait_ge(tt, ci * TPC)
            if i == 0:
                if ci >= 1:
                    e.wait_ge(ss, ci * SPC)
                e.memset(S_sb[:, 0:256], 0.0).then_inc(vv, 1)
                e.memset(S_sb[:, 256:512], 0.0).then_inc(vv, 1)
                e.memset(S_f32[:, 0:256], 0.0).then_inc(vv, 1)
                e.memset(S_f32[:, 256:512], 0.0).then_inc(vv, 1)
            e.wait_ge(tt, bt + 1)
            e.tensor_tensor(A[:, 0:128], PS[2][:, 0:128], g["stril_s"][:, :],
                            ALU.mult).then_inc(vv, 1)
            e.wait_ge(tt, bt + 2)
            e.tensor_tensor(At[:, 0:128], PS[3][:, 0:128], g["striu_s"][:, :],
                            ALU.mult).then_inc(vv, 1)
            e.wait_ge(tt, bt + 3)
            e.tensor_tensor(attnT[:, :], PS[4][:, 0:128], g["triui_s"][:, :],
                            ALU.mult).then_inc(vv, 1)
            for li in range(5):
                e.wait_ge(tt, bt + 4 + 2 * li)
                e.tensor_copy(A[:, (li + 1) * 128:(li + 2) * 128],
                              PS[2][:, 0:128]).then_inc(vv, 1)
                e.wait_ge(tt, bt + 5 + 2 * li)
                e.tensor_copy(At[:, (li + 1) * 128:(li + 2) * 128],
                              PS[3][:, 0:128]).then_inc(vv, 1)
            e.wait_ge(tt, bt + 14)
            e.tensor_copy(A[:, 6 * 128:7 * 128], PS[2][:, 0:128]).then_inc(vv, 1)
            e.wait_ge(vv, bv + 2)
            e.tensor_tensor(Tt[:, 0:128], g["ident_s"][:, :], At[:, 0:128],
                            ALU.subtract).then_inc(vv, 1)
            for j in range(6):
                e.wait_ge(tt, bt + 15 + j)
                e.wait_ge(vv, bv + 15 + j)
                e.tensor_tensor(Tt[:, ((j + 1) % 2) * 128:((j + 1) % 2) * 128 + 128],
                                PS[4][:, 0:128],
                                Tt[:, (j % 2) * 128:(j % 2) * 128 + 128],
                                ALU.add).then_inc(vv, 1)
            e.wait_ge(ss, ci * SPC + 1)
            e.wait_ge(tt, bt + 23)
            e.tensor_tensor(u2_sb[:, :], u_sb[:, :], PS[7][:, 0:256],
                            ALU.subtract).then_inc(vv, 1)
            e.wait_ge(tt, bt + 25)
            e.tensor_tensor(S_f32[:, 0:256], S_f32[:, 0:256], PS[0][:, 0:256],
                            ALU.add).then_inc(vv, 1)
            e.tensor_tensor(S_f32[:, 256:512], S_f32[:, 256:512], PS[1][:, 0:256],
                            ALU.add).then_inc(vv, 1)

    @block.scalar
    def _(e):
        for ci in range(64):
            p = ci % 2
            bt = ci * TPC
            e.wait_ge(tt, bt + 21)
            e.activation(u_sb[:, :], PS[5][:, 128:384], AF.Identity).then_inc(ss, 1)
            e.wait_ge(tt, bt + 22)
            e.activation(wT_sb[:, :], PS[6][:, 0:256], AF.Identity).then_inc(ss, 1)
            e.wait_ge(tt, bt + 24)
            if ci >= 2:
                e.wait_ge(sgs[ci % 2], 16 * (ci // 2))
            e.activation(ot[:, p * 256:(p + 1) * 256], PS[7][:, 256:512],
                         AF.Identity).then_inc(ss, 1)
            e.wait_ge(vv, vb(ci) + 24)
            e.activation(S_sb[:, 0:256], S_f32[:, 0:256], AF.Identity).then_inc(ss, 1)
            e.activation(S_sb[:, 256:512], S_f32[:, 256:512], AF.Identity).then_inc(ss, 1)

    @block.gpsimd
    def _(e):
        for ci in range(64):
            bh, i = ci // 32, ci % 32
            p = ci % 2
            e.wait_ge(ss, ci * SPC + 3)
            e.dma_start(
                out=g["delta_d"][bh * 256:(bh + 1) * 256, i * 128:(i + 1) * 128]
                .rearrange("(c q) t -> q c t", c=2),
                in_=ot[:, p * 256:(p + 1) * 256].rearrange("q (c t) -> q c t", c=2)
            ).then_inc(sgs[p], 16)
        e.wait_ge(sg0, 16 * 32)
        e.wait_ge(sg1, 16 * 32)


def _s5_stats(bx, block):
    nc, g = bx.nc, bx.g
    PS = bx.PS
    br = bx.sb("s5_br", [128, 2 * L])
    sq = bx.sb("s5_sq", [128, 2 * L])
    sel_s = bx.sb("s5_sel", [128, 16 * 64])
    meansb = bx.sb("s5_mean", [8, 2 * 512], F32)
    m2sb = bx.sb("s5_m2", [8, 2 * 512], F32)
    varsb = bx.sb("s5_var", [8, 2 * 512], F32)
    stdsb = bx.sb("s5_std", [8, 2 * 512], F32)
    lv0 = bx.sem("lv0")
    lv1 = bx.sem("lv1")
    lvs = (lv0, lv1)
    tt = bx.sem("t")
    vv = bx.sem("v")
    ss = bx.sem("s")
    sg0 = bx.sem("g0")
    sg1 = bx.sem("g1")
    sgp = (sg0, sg1)
    cc_sem = bx.sem("cc")
    srcs = [g["firs_d"], g["firl_d"], g["delta_d"], g["vpost_d"]]

    @block.sync
    def _(e):
        e.dma_start(out=sel_s[:, :], in_=g["sel_c"]).then_inc(lv0, 16)
        for j in range(16):
            branch, cc = j // 4, j % 4
            p = j % 2
            if j >= 2:
                e.wait_ge(tt, (j - 1) * 8)
            e.dma_start(out=br[:, p * L:(p + 1) * L],
                        in_=srcs[branch][cc * 128:(cc + 1) * 128, :]).then_inc(lvs[p], 16)

    @block.scalar
    def _(e):
        for j in range(16):
            p = j % 2
            extra = 16 if p == 0 else 0
            e.wait_ge(lvs[p], extra + 16 * (j // 2 + 1))
            if j >= 2:
                e.wait_ge(tt, (j - 1) * 8)
            e.activation(sq[:, p * L:(p + 1) * L], br[:, p * L:(p + 1) * L],
                         AF.Square).then_inc(ss, 1)
        for n in range(8):
            e.wait_ge(vv, n * 4 + 4)
            e.activation(stdsb[:, (n % 2) * 512:(n % 2) * 512 + 512],
                         varsb[:, (n % 2) * 512:(n % 2) * 512 + 512],
                         AF.Sqrt).then_inc(ss, 1)

    @block.tensor
    def _(e):
        for j in range(16):
            branch, cc = j // 4, j % 4
            p = j % 2
            head = cc // 2
            bh = branch * 2 + head
            extra = 16 if p == 0 else 0
            e.wait_ge(lvs[p], extra + 16 * (j // 2 + 1))
            e.wait_ge(ss, j + 1)
            for n in range(8):
                e.matmul(PS[n][:64, :], sel_s[:, bh * 64:bh * 64 + 64],
                         br[:, p * L + n * 512:p * L + n * 512 + 512],
                         start=(j == 0), stop=False)
                mi = e.matmul(PS[n][:64, :], sel_s[:, (8 + bh) * 64:(8 + bh) * 64 + 64],
                              sq[:, p * L + n * 512:p * L + n * 512 + 512],
                              start=False, stop=(j == 15))
                mi.then_inc(tt, 1)

    @block.vector
    def _(e):
        for n in range(8):
            pn = n % 2
            e.wait_ge(tt, 16 * 8)
            if n >= 2:
                e.wait_ge(sgp[n % 2], 32 * (n // 2))
            e.tensor_copy(meansb[:, pn * 512:(pn + 1) * 512],
                          PS[n][:8, :]).then_inc(vv, 1)
            e.wait_ge(vv, n * 4 + 1)
            e.tensor_tensor(m2sb[:, pn * 512:(pn + 1) * 512],
                            meansb[:, pn * 512:(pn + 1) * 512],
                            meansb[:, pn * 512:(pn + 1) * 512], ALU.mult).then_inc(vv, 1)
            e.wait_ge(vv, n * 4 + 2)
            e.scalar_tensor_tensor(varsb[:, pn * 512:(pn + 1) * 512],
                                   m2sb[:, pn * 512:(pn + 1) * 512], -1.0,
                                   PS[n][32:40, :], ALU.mult, ALU.add).then_inc(vv, 1)
            e.wait_ge(vv, n * 4 + 3)
            e.tensor_scalar(varsb[:, pn * 512:(pn + 1) * 512],
                            varsb[:, pn * 512:(pn + 1) * 512], 1e-6, None,
                            ALU.max).then_inc(vv, 1)

    @block.gpsimd
    def _(e):
        for n in range(8):
            pn = n % 2
            e.wait_ge(vv, n * 4 + 1)
            sh, nn = n // 4, n % 4
            e.dma_start(out=g["stats_my"][sh * 16:sh * 16 + 8, nn * 512:(nn + 1) * 512],
                        in_=meansb[:, pn * 512:(pn + 1) * 512]).then_inc(sgp[pn], 16)
            e.wait_ge(ss, 16 + n + 1)
            e.dma_start(out=g["stats_my"][sh * 16 + 8:sh * 16 + 16, nn * 512:(nn + 1) * 512],
                        in_=stdsb[:, pn * 512:(pn + 1) * 512]).then_inc(sgp[pn], 16)
        e.wait_ge(sg0, 32 * 4)
        e.wait_ge(sg1, 32 * 4)
        e.collective_compute("AllGather", ALU.bypass,
                             replica_groups=[[0, 1], [2, 3], [4, 5], [6, 7]],
                             ins=[g["stats_my"].opt()],
                             outs=[g["stats_all"].opt()]).then_inc(cc_sem, 1)
        e.wait_ge(cc_sem, 1)


def _s6a_lnmy(bx, block):
    _emit_ln(bx, bx.g and block, bx.g["xmy_in"], bx.g["ln1my_d"], 16)


def _s6_gate(bx, block):
    nc, g = bx.nc, bx.g
    PS = bx.PS
    gin = bx.sb("s6_gin", [128, 8 * 2048])
    gstat = bx.sb("s6_gstat", [64, 2048], F32)
    gstatb = bx.sb("s6_gstatb", [64, 2048])
    w1b = bx.sb("s6_w1", [128, 128 * 128])
    w1sb = bx.sb("s6_w1s", [64, 16 * 128])
    w2b = bx.sb("s6_w2", [128, 16 * 16])
    w2p = bx.sb("s6_w2p", [128, 16 * 16])
    b1row = bx.sb("s6_b1r", [1, 2048], F32)
    bias1 = bx.sb("s6_b1c", [128, 16], F32)
    b2row = bx.sb("s6_b2r", [1, 16], F32)
    b2prow = bx.sb("s6_b2pr", [1, 16], F32)
    b2pcol = bx.sb("s6_b2pc", [16, 1], F32)
    gltrow = bx.sb("s6_glt", [1, 4], F32)
    sp = bx.sb("s6_sp", [1, 32], F32)
    one1 = bx.sb("s6_one", [1, 1], F32)
    invtb = bx.sb("s6_invtb", [1, 16])
    invt_tile = bx.sb("s6_invt", [128, 16])
    logits = bx.sb("s6_log", [16, 2048], F32)
    lt_ = bx.sb("s6_lt", [128, 2 * 16], F32)
    wgt = bx.sb("s6_wgt", [128, 2 * 16], F32)
    mxc = bx.sb("s6_mxc", [128, 2 * 12], F32)
    hbuf = bx.sb("s6_h", [128, 16 * 512])
    lw = bx.sem("lw")
    lt = bx.sem("lt")
    tt = bx.sem("t")
    vv = bx.sem("v")
    ss = bx.sem("s")
    sg0 = bx.sem("g0")
    sg1 = bx.sem("g1")
    sgs = (sg0, sg1)
    cc_sem = bx.sem("cc")
    dsem = bx.sem("dbg")
    LWT = 16 * 164

    @block.sync
    def _(e):
        for c in range(8):
            for n in range(4):
                e.dma_start(out=gin[:, c * 2048 + n * 512:c * 2048 + n * 512 + 512],
                            in_=g["ln1my_d"][n * 512:(n + 1) * 512, c * 128:(c + 1) * 128],
                            transpose=True).then_inc(lt, 16)
        e.wait_ge(lt, 16 * 32)
        e.nop().then_inc(lt, 1)

    @block.gpsimd
    def _(e):
        e.dma_start(out=gstat[:, :], in_=g["stats_all"]).then_inc(lw, 16)
        e.dma_start(out=b1row[:, :], in_=g["b1_in"]).then_inc(lw, 16)
        e.dma_start(out=b2row[:, :], in_=g["b2_in"]).then_inc(lw, 16)
        e.dma_start(out=gltrow[:, :], in_=g["glt_in"]).then_inc(lw, 16)
        for kk in range(16):
            e.dma_start(out=w2b[:, kk * 16:(kk + 1) * 16],
                        in_=g["w2_in"][kk * 128:(kk + 1) * 128, :]).then_inc(lw, 16)
        for kk in range(8):
            for m in range(16):
                e.dma_start(out=w1b[:, (kk * 16 + m) * 128:(kk * 16 + m) * 128 + 128],
                            in_=g["w1_in"][kk * 128:(kk + 1) * 128, m * 128:(m + 1) * 128]
                            ).then_inc(lw, 16)
        for m in range(16):
            e.dma_start(out=w1sb[:, m * 128:(m + 1) * 128],
                        in_=g["w1s_in"][:, m * 128:(m + 1) * 128]).then_inc(lw, 16)
        for i in range(16 if S6_LEVEL >= 4 else 0):
            e.wait_ge(vv, 29 + 6 * i + 6)
            e.dma_start(out=g["wgt_my"][i * 128:(i + 1) * 128, :],
                        in_=wgt[:, (i % 2) * 16:(i % 2) * 16 + 16]).then_inc(sgs[i % 2], 16)
        if S6_LEVEL >= 4:
            e.wait_ge(sg0, 16 * 8)
            e.wait_ge(sg1, 16 * 8)
        if S6_LEVEL >= 4:
            e.collective_compute("AllGather", ALU.bypass,
                                 replica_groups=[[0, 1], [2, 3], [4, 5], [6, 7]],
                                 ins=[g["wgt_my"].opt()],
                                 outs=[g["wgt_all"].opt()]).then_inc(cc_sem, 1)
            e.wait_ge(cc_sem, 1)
        if DEV_TAPS:
            dbg, orw = g["dbg_d"], g["oraw_d"]
            e.dma_start(out=dbg[:, 0:2048], in_=gin[:, 0:2048]).then_inc(dsem, 16)
            e.dma_start(out=dbg[:, 2048:2560], in_=hbuf[:, 0:512]).then_inc(dsem, 16)
            e.dma_start(out=dbg[:, 2560:2688], in_=w1b[:, 0:128]).then_inc(dsem, 16)
            e.dma_start(out=dbg[:, 2688:2944], in_=w2p[:, :]).then_inc(dsem, 16)
            e.dma_start(out=orw[0:16, 0:2048], in_=logits[:16, :]).then_inc(dsem, 16)
            e.dma_start(out=orw[16:144, 0:32], in_=wgt[:, :]).then_inc(dsem, 16)
            e.dma_start(out=orw[144:272, 0:32], in_=lt_[:, :]).then_inc(dsem, 16)
            e.dma_start(out=orw[272:400, 0:16], in_=bias1[:, :]).then_inc(dsem, 16)
            e.dma_start(out=orw[400:401, 0:28], in_=sp[:1, 0:28]).then_inc(dsem, 16)
            e.dma_start(out=orw[401:465, 0:2048], in_=gstat[:, :]).then_inc(dsem, 16)
            e.dma_start(out=orw[465:593, 0:24], in_=mxc[:, :]).then_inc(dsem, 16)
            e.wait_ge(dsem, 16 * 11)

    @block.scalar
    def _(e):
        if S6_LEVEL < 0:
            return
        e.wait_ge(lw, LWT)
        e.wait_ge(vv, 1)
        e.activation(sp[:1, 0:4], gltrow[:1, :], AF.Abs).then_inc(ss, 1)
        e.wait_ge(ss, 1)
        e.activation(sp[:1, 4:8], sp[:1, 0:4], AF.Exp, scale=-1.0).then_inc(ss, 1)
        e.wait_ge(ss, 2)
        e.activation(sp[:1, 8:12], sp[:1, 4:8], AF.Ln, bias=one1[:1, :1]).then_inc(ss, 1)
        e.wait_ge(ss, 3)
        e.activation(sp[:1, 12:16], gltrow[:1, :], AF.Relu).then_inc(ss, 1)
        for m in range(16):
            e.wait_ge(tt, m + 1)
            e.activation(bias1[:, m:m + 1], PS[6][:, 0:1], AF.Identity).then_inc(ss, 1)
        if S6_LEVEL >= 1:
            for n in range(4):
                for m in range(16):
                    j = n * 16 + m
                    e.wait_ge(tt, 18 + n * 17 + m + 1)
                    if n >= 1:
                        e.wait_ge(tt, 35 + 17 * (n - 1))
                    e.activation(hbuf[:, m * 512:(m + 1) * 512],
                                 PS[j % 2][:, :], AF.Gelu,
                                 bias=bias1[:, m:m + 1]).then_inc(ss, 1)
        if S6_LEVEL >= 3:
            for i in range(16):
                p = i % 2
                e.wait_ge(vv, 29 + 6 * i + 3)
                e.activation(wgt[:, p * 16:(p + 1) * 16], wgt[:, p * 16:(p + 1) * 16],
                             AF.Exp).then_inc(ss, 1)

    @block.vector
    def _(e):
        if S6_LEVEL < 0:
            return
        e.memset(one1[:, :], 1.0).then_inc(vv, 1)
        e.wait_ge(ss, 4)
        e.tensor_tensor(sp[:1, 16:20], sp[:1, 8:12], sp[:1, 12:16], ALU.add).then_inc(vv, 1)
        e.wait_ge(vv, 2)
        e.tensor_scalar(sp[:1, 20:24], sp[:1, 16:20], 1e-4, None, ALU.add).then_inc(vv, 1)
        e.wait_ge(vv, 3)
        e.reciprocal(sp[:1, 24:28], sp[:1, 20:24]).then_inc(vv, 1)
        e.wait_ge(vv, 4)
        e.tensor_copy(invtb[:1, :],
                      sp[:1, 24:28].rearrange("p f -> p f ()").broadcast_to([1, 4, 4])
                      ).then_inc(vv, 1)
        e.wait_ge(tt, 17)
        e.tensor_copy(invt_tile[:, :], PS[5][:, 0:16]).then_inc(vv, 1)
        e.wait_ge(vv, 6)
        e.tensor_tensor(b2prow[:1, :], b2row[:1, :], invtb[:1, :], ALU.mult).then_inc(vv, 1)
        e.wait_ge(tt, 18)
        e.tensor_copy(b2pcol[:, :], PS[7][:16, 0:1]).then_inc(vv, 1)
        for kk in range(16):
            e.tensor_tensor(w2p[:, kk * 16:(kk + 1) * 16], w2b[:, kk * 16:(kk + 1) * 16],
                            invt_tile[:, :], ALU.mult).then_inc(vv, 1)
        e.wait_ge(lw, LWT)
        e.tensor_copy(gstatb[:, :], gstat[:, :]).then_inc(vv, 1)
        for n in range(4):
            if S6_LEVEL >= 2:
                e.wait_ge(tt, 35 + 17 * n)
                e.tensor_scalar(logits[:, n * 512:(n + 1) * 512], PS[2][:16, :],
                                b2pcol[:16, 0:1], None, ALU.add).then_inc(vv, 1)
            else:
                e.nop().then_inc(vv, 1)
        for i in range(16 if S6_LEVEL >= 3 else 0):
            p = i % 2
            b0 = 29 + 6 * i
            e.wait_ge(tt, 87 + i)
            e.tensor_copy(lt_[:, p * 16:(p + 1) * 16],
                          PS[3 + (i % 2)][:, 0:16]).then_inc(vv, 1)
            e.wait_ge(vv, b0 + 1)
            e.tensor_reduce(mxc[:, p * 12:p * 12 + 4],
                            lt_[:, p * 16:(p + 1) * 16].rearrange("q (h m) -> q h m", h=4),
                            axis=AX, op=ALU.max).then_inc(vv, 1)
            e.wait_ge(vv, b0 + 2)
            if i >= 2:
                e.wait_ge(sgs[i % 2], 16 * (i // 2))
            e.tensor_tensor(wgt[:, p * 16:(p + 1) * 16].rearrange("q (h m) -> q h m", h=4),
                            lt_[:, p * 16:(p + 1) * 16].rearrange("q (h m) -> q h m", h=4),
                            mxc[:, p * 12:p * 12 + 4].rearrange("q h -> q h ()")
                            .broadcast_to([128, 4, 4]),
                            ALU.subtract).then_inc(vv, 1)
            e.wait_ge(ss, 84 + i + 1)
            e.tensor_reduce(mxc[:, p * 12 + 4:p * 12 + 8],
                            wgt[:, p * 16:(p + 1) * 16].rearrange("q (h m) -> q h m", h=4),
                            axis=AX, op=ALU.add).then_inc(vv, 1)
            e.wait_ge(vv, b0 + 4)
            e.reciprocal(mxc[:, p * 12 + 8:p * 12 + 12],
                         mxc[:, p * 12 + 4:p * 12 + 8]).then_inc(vv, 1)
            e.wait_ge(vv, b0 + 5)
            e.tensor_tensor(wgt[:, p * 16:(p + 1) * 16].rearrange("q (h m) -> q h m", h=4),
                            wgt[:, p * 16:(p + 1) * 16].rearrange("q (h m) -> q h m", h=4),
                            mxc[:, p * 12 + 8:p * 12 + 12].rearrange("q h -> q h ()")
                            .broadcast_to([128, 4, 4]),
                            ALU.mult).then_inc(vv, 1)

    @block.tensor
    def _(e):
        if S6_LEVEL < 0:
            return
        e.wait_ge(lw, LWT)
        for m in range(16):
            if m >= 1:
                e.wait_ge(ss, 4 + m)
            e.transpose(PS[6][:, 0:1], b1row[:1, m * 128:(m + 1) * 128],
                        g["identf_s"][:1, :1]).then_inc(tt, 1)
        e.wait_ge(vv, 5)
        e.matmul(PS[5][:, 0:16], g["ones_s"][:1, :], invtb[:1, :],
                 start=True, stop=True).then_inc(tt, 1)
        e.wait_ge(vv, 7)
        e.transpose(PS[7][:16, 0:1], b2prow[:1, :], g["identf_s"][:1, :1]).then_inc(tt, 1)
        e.wait_ge(vv, 25)
        e.wait_ge(lt, 16 * 32 + 1)
        for n in range(4 if S6_LEVEL >= 1 else 0):
            for m in range(16):
                j = n * 16 + m
                if j >= 2:
                    e.wait_ge(ss, 19 + j)
                for kk in range(8):
                    e.matmul(PS[j % 2][:, :],
                             w1b[:, (kk * 16 + m) * 128:(kk * 16 + m) * 128 + 128],
                             gin[:, kk * 2048 + n * 512:kk * 2048 + n * 512 + 512],
                             start=(kk == 0), stop=False)
                mi = e.matmul(PS[j % 2][:, :], w1sb[:, m * 128:(m + 1) * 128],
                              gstatb[:, n * 512:(n + 1) * 512],
                              start=False, stop=True)
                mi.then_inc(tt, 1)
            e.wait_ge(ss, 20 + n * 16 + 16)
            if n >= 1:
                e.wait_ge(vv, 25 + n)
            if S6_LEVEL >= 2:
                for kk in range(16):
                    mi = e.matmul(PS[2][:16, :], w2p[:, kk * 16:(kk + 1) * 16],
                                  hbuf[:, kk * 512:(kk + 1) * 512],
                                  start=(kk == 0), stop=(kk == 15))
                mi.then_inc(tt, 1)
            else:
                e.nop().then_inc(tt, 1)
        for i in range(16 if S6_LEVEL >= 3 else 0):
            e.wait_ge(vv, 25 + (i // 4) + 1)
            if i >= 2:
                e.wait_ge(vv, 29 + 6 * (i - 2) + 1)
            e.transpose(PS[3 + (i % 2)][:, 0:16], logits[:16, i * 128:(i + 1) * 128],
                        g["identf_s"][:16, :16]).then_inc(tt, 1)


def _s7_combine(bx, block):
    nc, g = bx.nc, bx.g
    PS = bx.PS
    NTI = 32
    brt = bx.sb("s7_brt", [128, 4 * CH])
    wgtile = bx.sb("s7_wgt", [128, 16], F32)
    wgT = bx.sb("s7_wgT", [16, 128], F32)
    w8 = bx.sb("s7_w8", [8, 128], F32)
    wcols = bx.sb("s7_wc", [128, 8], F32)
    wselb = bx.sb("s7_sel", [16, 8], F32)
    ot = bx.sb("s7_ot", [128, CH], F32)
    sq = bx.sb("s7_sq", [128, CH], F32)
    ssq = bx.sb("s7_ssq", [128, 2], F32)
    rms0 = bx.sb("s7_rms", [128, 2], F32)
    rinv = bx.sb("s7_rinv", [128, 2], F32)
    eps1 = bx.sb("s7_eps", [128, 1], F32)
    od_sb = bx.sb("s7_od", [128, 4 * 128])
    ld = bx.sem("ld")
    lw = bx.sem("lw")
    tt = bx.sem("t")
    vv = bx.sem("v")
    ss = bx.sem("s")
    st = bx.sem("st")
    cc = bx.sem("cc")
    VPT, TPT, SPT = 15, 7, 5
    brs = ("firs_d", "firl_d", "delta_d", "vpost_d")

    @block.sync
    def _(e):
        for i in range(NTI):
            if i > 0:
                e.wait_ge(vv, 1 + VPT * (i - 1) + 11)
                e.wait_ge(tt, TPT * (i - 1) + 1)
            e.dma_start(out=wgtile[:, :],
                        in_=g["wgt_all"][i * 128:(i + 1) * 128, :]).then_inc(ld, 16)
            for bi, nm in enumerate(brs):
                e.dma_start(out=brt[:, bi * CH:(bi + 1) * CH],
                            in_=g[nm][:, i * 128:(i + 1) * 128],
                            transpose=True).then_inc(ld, 16)

    @block.tensor
    def _(e):
        e.wait_ge(lw, 16)
        for i in range(NTI):
            e.wait_ge(ld, 80 * (i + 1))
            e.transpose(PS[0][:16, 0:128], wgtile[:, :],
                        g["identf_s"][:, :]).then_inc(tt, 1)
            e.wait_ge(vv, 1 + VPT * i + 1)
            e.matmul(PS[1][:8, 0:128], wselb[:, :], wgT[:, :],
                     start=True, stop=True).then_inc(tt, 1)
            e.wait_ge(vv, 1 + VPT * i + 2)
            e.transpose(PS[2][:, 0:8], w8[:, :],
                        g["identf_s"][:8, :8]).then_inc(tt, 1)
            e.wait_ge(vv, 1 + VPT * i + 15)
            e.wait_ge(ss, SPT * i)
            for c in range(4):
                e.transpose(PS[3 + c][:, 0:128], ot[:, c * 128:(c + 1) * 128],
                            g["identf_s"][:, :]).then_inc(tt, 1)

    @block.vector
    def _(e):
        e.memset(eps1[:, :], 1e-5).then_inc(vv, 1)
        for i in range(NTI):
            base = 1 + VPT * i
            e.wait_ge(tt, TPT * i + 1)
            e.tensor_copy(wgT[:, :], PS[0][:16, 0:128]).then_inc(vv, 1)
            e.wait_ge(tt, TPT * i + 2)
            e.tensor_copy(w8[:, :], PS[1][:8, 0:128]).then_inc(vv, 1)
            e.wait_ge(tt, TPT * i + 3)
            e.tensor_copy(wcols[:, :], PS[2][:, 0:8]).then_inc(vv, 1)
            if i > 0:
                e.wait_ge(tt, TPT * (i - 1) + 7)
            e.wait_ge(vv, base + 3)
            for h in range(2):
                e.tensor_scalar(ot[:, h * 256:(h + 1) * 256],
                                brt[:, h * 256:(h + 1) * 256],
                                wcols[:, 4 * h:4 * h + 1], None,
                                ALU.mult).then_inc(vv, 1)
            for bi in range(1, 4):
                for h in range(2):
                    e.wait_ge(vv, base + 4 + 2 * (bi - 1) + h)
                    e.scalar_tensor_tensor(ot[:, h * 256:(h + 1) * 256],
                                           brt[:, bi * CH + h * 256:bi * CH + (h + 1) * 256],
                                           wcols[:, 4 * h + bi:4 * h + bi + 1],
                                           ot[:, h * 256:(h + 1) * 256],
                                           ALU.mult, ALU.add).then_inc(vv, 1)
            e.wait_ge(vv, base + 11)
            e.tensor_tensor(sq[:, :], ot[:, :], ot[:, :], ALU.mult).then_inc(vv, 1)
            e.wait_ge(vv, base + 12)
            e.tensor_reduce(ssq[:, :], sq[:, :].rearrange("p (h c) -> p h c", h=2),
                            axis=AX, op=ALU.add).then_inc(vv, 1)
            e.wait_ge(ss, SPT * i + 1)
            e.reciprocal(rinv[:, :], rms0[:, :]).then_inc(vv, 1)
            e.wait_ge(vv, base + 14)
            e.tensor_tensor(ot[:, :].rearrange("p (h c) -> p h c", h=2),
                            ot[:, :].rearrange("p (h c) -> p h c", h=2),
                            rinv[:, :].rearrange("p h -> p h ()").broadcast_to([128, 2, 256]),
                            ALU.mult).then_inc(vv, 1)

    @block.scalar
    def _(e):
        for i in range(NTI):
            e.wait_ge(vv, 1 + VPT * i + 13)
            e.activation(rms0[:, :], ssq[:, :], AF.Sqrt, bias=eps1[:, 0:1],
                         scale=1.0 / 256.0).then_inc(ss, 1)
            for c in range(4):
                e.wait_ge(tt, TPT * i + 4 + c)
                if i > 0 and c == 0:
                    e.wait_ge(st, 64 * i)
                e.activation(od_sb[:, c * 128:(c + 1) * 128], PS[3 + c][:, 0:128],
                             AF.Identity).then_inc(ss, 1)

    @block.gpsimd
    def _(e):
        e.dma_start(out=wselb[:, :], in_=g["wsel_in"]).then_inc(lw, 16)
        for i in range(NTI):
            for c in range(4):
                e.wait_ge(ss, SPT * i + 2 + c)
                e.dma_start(out=g["o_my_d"][c * 128:(c + 1) * 128, i * 128:(i + 1) * 128],
                            in_=od_sb[:, c * 128:(c + 1) * 128]).then_inc(st, 16)
        e.wait_ge(st, 16 * 4 * NTI)
        if DEV_TAPS:
            e.collective_compute("AllGather", ALU.bypass,
                                 replica_groups=[[0, 1], [2, 3], [4, 5], [6, 7]],
                                 ins=[g["o_my_d"].opt()],
                                 outs=[g["o_all_d"].opt()]).then_inc(cc, 1)
            e.wait_ge(cc, 1)


def _s8_wo(bx, block):
    nc, g = bx.nc, bx.g
    PS = bx.PS
    NTA, NTB = 32, 16
    wotmp = bx.sb("s8_wot", [128, H], F32)
    wob = bx.sb("s8_wob", [128, 4 * H])
    oc = bx.sb("s8_oc", [128, 4 * 128])
    at_sb = bx.sb("s8_at", [128, H])
    onwcb = bx.sb("s8_onwc", [128, 2], F32)
    mAb = bx.sb("s8_mA", [128, 1], F32)
    mBb = bx.sb("s8_mB", [128, 1], F32)
    epsb = bx.sb("s8_eps", [128, 1], F32)
    xa = bx.sb("s8_xa", [128, H])
    xb = bx.sb("s8_xb", [128, H])
    xm = bx.sb("s8_xm", [128, H], F32)
    xcA = bx.sb("s8_xcA", [128, H], F32)
    xcB = bx.sb("s8_xcB", [128, H], F32)
    sqb = bx.sb("s8_sq", [128, H], F32)
    lnA = bx.sb("s8_lnA", [128, H], F32)
    lnB = bx.sb("s8_lnB", [128, H], F32)
    x2t = bx.sb("s8_x2", [128, H], F32)
    colsb = bx.sb("s8_cols", [128, 8], F32)
    rstdb = bx.sb("s8_rstd", [128, 2], F32)
    ld = bx.sem("ld")
    tt = bx.sem("t")
    vv = bx.sem("v")
    ss = bx.sem("s")
    st = bx.sem("st")
    cc = bx.sem("cc")
    LDA = 128
    LDA_END = LDA + 64 * NTA
    VB0 = 5 + 2 * NTA

    def vbB(j):
        return VB0 + 10 * j

    @block.sync
    def _(e):
        e.dma_start(out=onwcb[:, 0:1], in_=g["onwc_in"][0:128, :]).then_inc(ld, 16)
        e.dma_start(out=onwcb[:, 1:2], in_=g["onwc_in"][128:256, :]).then_inc(ld, 16)
        e.dma_start(out=mAb[:, :], in_=g["mA_in"]).then_inc(ld, 16)
        e.dma_start(out=mBb[:, :], in_=g["mB_in"]).then_inc(ld, 16)
        for c in range(4):
            if c > 0:
                e.wait_ge(vv, c + 1)
            e.dma_start(out=wotmp[:, :],
                        in_=g["wo_in"][c * 128:(c + 1) * 128, :]).then_inc(ld, 16)
        e.wait_ge(ld, LDA)
        for i in range(NTA):
            if i > 0:
                e.wait_ge(tt, 8 * i)
            for c in range(4):
                e.dma_start(out=oc[:, c * 128:(c + 1) * 128],
                            in_=g["o_my_d"][c * 128:(c + 1) * 128,
                                            i * 128:(i + 1) * 128]).then_inc(ld, 16)
        e.wait_ge(cc, 1)
        for j in range(NTB):
            if j > 0:
                e.wait_ge(vv, vbB(j - 1) + 10)
            e.dma_start(out=xa[:, :],
                        in_=g["attnsum_d"][j * 128:(j + 1) * 128, :]).then_inc(ld, 16)
            e.dma_start(out=xb[:, :],
                        in_=g["attnsum_d"][LH + j * 128:LH + (j + 1) * 128, :]).then_inc(ld, 16)
            e.dma_start(out=xm[:, :],
                        in_=g["xmy_in"][j * 128:(j + 1) * 128, :]).then_inc(ld, 16)

    @block.tensor
    def _(e):
        e.wait_ge(vv, 5)
        for i in range(NTA):
            e.wait_ge(ld, LDA + 64 * (i + 1))
            if i > 0:
                e.wait_ge(vv, 5 + 2 * (i - 1) + 2)
            for n in range(2):
                for c in range(4):
                    mi = e.matmul(PS[n][:, :], oc[:, c * 128:(c + 1) * 128],
                                  wob[:, c * H + n * 512:c * H + n * 512 + 512],
                                  start=(c == 0), stop=(c == 3))
                    mi.then_inc(tt, 1)

    @block.vector
    def _(e):
        e.memset(epsb[:, :], 1e-5).then_inc(vv, 1)
        for c in range(4):
            e.wait_ge(ld, 64 + 16 * (c + 1))
            e.tensor_scalar(wob[:, c * H:(c + 1) * H], wotmp[:, :],
                            onwcb[:, c % 2:c % 2 + 1], None,
                            ALU.mult).then_inc(vv, 1)
        for i in range(NTA):
            e.wait_ge(tt, 8 * i + 4)
            if i > 0:
                e.wait_ge(st, 16 * i)
            e.tensor_copy(at_sb[:, 0:512], PS[0][:, :]).then_inc(vv, 1)
            e.wait_ge(tt, 8 * i + 8)
            e.tensor_copy(at_sb[:, 512:1024], PS[1][:, :]).then_inc(vv, 1)
        for j in range(NTB):
            b0 = vbB(j)
            e.wait_ge(ld, LDA_END + 48 * (j + 1))
            e.tensor_scalar(lnB[:, :], xa[:, :], mAb[:, 0:1], None,
                            ALU.mult).then_inc(vv, 1)
            e.wait_ge(vv, b0 + 1)
            e.scalar_tensor_tensor(lnB[:, :], xb[:, :], mBb[:, 0:1], lnB[:, :],
                                   ALU.mult, ALU.add).then_inc(vv, 1)
            e.wait_ge(vv, b0 + 2)
            e.tensor_reduce(colsb[:, 0:1],
                            lnB[:, :].rearrange("p (g c) -> p g c", g=1),
                            axis=AX, op=ALU.add).then_inc(vv, 1)
            e.wait_ge(vv, b0 + 3)
            e.tensor_scalar(colsb[:, 1:2], colsb[:, 0:1], -1.0 / H, None,
                            ALU.mult).then_inc(vv, 1)
            e.wait_ge(vv, b0 + 4)
            e.tensor_scalar(xcA[:, :], lnB[:, :], colsb[:, 1:2], None,
                            ALU.add).then_inc(vv, 1)
            e.wait_ge(vv, b0 + 5)
            e.tensor_tensor(sqb[:, :], xcA[:, :], xcA[:, :],
                            ALU.mult).then_inc(vv, 1)
            e.wait_ge(vv, b0 + 6)
            e.tensor_reduce(colsb[:, 2:3],
                            sqb[:, :].rearrange("p (g c) -> p g c", g=1),
                            axis=AX, op=ALU.add).then_inc(vv, 1)
            e.wait_ge(ss, j + 1)
            e.reciprocal(colsb[:, 3:4], rstdb[:, 0:1]).then_inc(vv, 1)
            e.wait_ge(vv, b0 + 8)
            e.tensor_scalar(lnA[:, :], xcA[:, :], colsb[:, 3:4], None,
                            ALU.mult).then_inc(vv, 1)
            e.wait_ge(vv, b0 + 9)
            e.wait_ge(st, 16 * NTA + 16 * j)
            e.tensor_tensor(x2t[:, :], lnA[:, :], xm[:, :],
                            ALU.add).then_inc(vv, 1)

    @block.scalar
    def _(e):
        for j in range(NTB):
            b0 = vbB(j)
            e.wait_ge(vv, b0 + 7)
            e.activation(rstdb[:, 0:1], colsb[:, 2:3], AF.Sqrt,
                         bias=epsb[:, 0:1], scale=1.0 / H).then_inc(ss, 1)

    @block.gpsimd
    def _(e):
        for i in range(NTA):
            e.wait_ge(vv, 5 + 2 * i + 2)
            e.dma_start(out=g["attnp_d"][i * 128:(i + 1) * 128, :],
                        in_=at_sb[:, :]).then_inc(st, 16)
        e.wait_ge(st, 16 * NTA)
        e.collective_compute("AllReduce", ALU.add,
                             replica_groups=[[0, 1], [2, 3], [4, 5], [6, 7]],
                             ins=[g["attnp_d"].opt()],
                             outs=[g["attnsum_d"].opt()]).then_inc(cc, 1)
        e.wait_ge(cc, 1)
        for j in range(NTB):
            e.wait_ge(vv, vbB(j) + 10)
            e.dma_start(out=g["x2_d"][j * 128:(j + 1) * 128, :],
                        in_=x2t[:, :]).then_inc(st, 16)
        e.wait_ge(st, 16 * NTA + 16 * NTB)

G8 = [[0, 1, 2, 3, 4, 5, 6, 7]]


def _s9a_dq(bx, block):
    nc, g = bx.nc, bx.g
    PS = bx.PS
    ccg = g["ccg"]
    wch = bx.sb("s9a_w", [128, 384], F32)
    am = bx.sb("s9a_am", [128, 12], F32)
    s4 = bx.sb("s9a_s", [128, 12], F32)
    r4 = bx.sb("s9a_r", [128, 12], F32)
    wr = bx.sb("s9a_wr", [128, 384], F32)
    wq = bx.sb("s9a_wq", [128, 384], F32)
    wug_tr = bx.sb("s9a_tru", [128, 6144])
    wd_tr = bx.sb("s9a_trd", [128, 3072])
    ld = bx.sem("ld")
    tt = bx.sem("t")
    vv = bx.sem("v")
    st = bx.sem("st")
    chunks = []
    for k in range(24):
        chunks.append(("wu_my_in", k, 128, "wuT_my"))
    for k in range(24):
        chunks.append(("wg_my_in", k, 128, "wgT_my"))
    for k in range(8):
        chunks.append(("wd_my_in", k, 384, "wdT_my"))
    TRb = [0]
    V = [0]
    for (_, _, F, _) in chunks:
        TRb.append(TRb[-1] + F // 128)
        V.append(V[-1] + 7 + F // 128)
    TRT = TRb[-1]

    def EV(m):
        ci = next(i for i in range(len(chunks)) if TRb[i] <= m < TRb[i + 1])
        return V[ci] + 7 + (m - TRb[ci]) + 1

    @block.sync
    def _(e):
        for ci, (src, k, F, _) in enumerate(chunks):
            if ci > 0:
                e.wait_ge(vv, V[ci - 1] + 4)
            e.dma_start(out=wch[:, 0:F],
                        in_=g[src][k * 128:(k + 1) * 128, 0:F]).then_inc(ld, 16)

    @block.vector
    def _(e):
        for ci, (src, k, F, dst) in enumerate(chunks):
            NB = F // 32
            v0 = V[ci]
            e.wait_ge(ld, 16 * (ci + 1))
            e.tensor_reduce(am[:, 0:NB],
                            wch[:, 0:F].rearrange("p (g c) -> p g c", g=NB),
                            axis=AX, op=ALU.max,
                            apply_absolute_value=True).then_inc(vv, 1)
            e.wait_ge(vv, v0 + 1)
            e.tensor_scalar(s4[:, 0:NB], am[:, 0:NB], 1.0 / 7.0, 1e-12,
                            ALU.mult, ALU.max).then_inc(vv, 1)
            e.wait_ge(vv, v0 + 2)
            e.reciprocal(r4[:, 0:NB], s4[:, 0:NB]).then_inc(vv, 1)
            e.wait_ge(vv, v0 + 3)
            e.tensor_tensor(wr[:, 0:F].rearrange("p (g c) -> p g c", g=NB),
                            wch[:, 0:F].rearrange("p (g c) -> p g c", g=NB),
                            r4[:, 0:NB].rearrange("p g -> p g ()").broadcast_to([128, NB, 32]),
                            ALU.mult).then_inc(vv, 1)
            e.wait_ge(vv, v0 + 4)
            e.tensor_scalar(wr[:, 0:F], wr[:, 0:F], MAGIC, -MAGIC,
                            ALU.add, ALU.add).then_inc(vv, 1)
            e.wait_ge(vv, v0 + 5)
            e.tensor_scalar(wr[:, 0:F], wr[:, 0:F], 7.0, -8.0,
                            ALU.min, ALU.max).then_inc(vv, 1)
            e.wait_ge(vv, v0 + 6)
            e.wait_ge(tt, TRb[ci])
            e.tensor_tensor(wq[:, 0:F].rearrange("p (g c) -> p g c", g=NB),
                            wr[:, 0:F].rearrange("p (g c) -> p g c", g=NB),
                            s4[:, 0:NB].rearrange("p g -> p g ()").broadcast_to([128, NB, 32]),
                            ALU.mult).then_inc(vv, 1)
            for f in range(F // 128):
                m = TRb[ci] + f
                e.wait_ge(tt, m + 1)
                if dst == "wuT_my":
                    sl = wug_tr[:, k * 128:(k + 1) * 128]
                elif dst == "wgT_my":
                    sl = wug_tr[:, 3072 + k * 128:3072 + (k + 1) * 128]
                else:
                    sl = wd_tr[:, f * 1024 + k * 128:f * 1024 + k * 128 + 128]
                e.tensor_copy(sl, PS[m % 3][:, 0:128]).then_inc(vv, 1)

    @block.tensor
    def _(e):
        for ci, (src, k, F, _) in enumerate(chunks):
            e.wait_ge(vv, V[ci] + 7)
            for f in range(F // 128):
                m = TRb[ci] + f
                if m >= 3:
                    e.wait_ge(vv, EV(m - 3))
                e.transpose(PS[m % 3][:, 0:128], wq[:, f * 128:(f + 1) * 128],
                            g["identf_s"][:, :]).then_inc(tt, 1)

    @block.gpsimd
    def _(e):
        e.wait_ge(vv, EV(TRb[48] - 1))
        e.dma_start(out=g["wugT_my"], in_=wug_tr[:, :]).then_inc(st, 16)
        e.wait_ge(vv, EV(TRT - 1))
        for f in range(3):
            e.dma_start(out=g["wdT_my"][f * 128:(f + 1) * 128, :],
                        in_=wd_tr[:, f * 1024:(f + 1) * 1024]).then_inc(st, 16)
        e.wait_ge(st, 16 * 4)
        e.collective_compute("AllGather", ALU.bypass, replica_groups=G8,
                             ins=[g["wugT_my"].opt()],
                             outs=[g["wugT_all"].opt()]).then_inc(ccg, 1)
        e.collective_compute("AllGather", ALU.bypass, replica_groups=G8,
                             ins=[g["wdT_my"].opt()],
                             outs=[g["wdT_all"].opt()]).then_inc(ccg, 1)
        e.wait_ge(ccg, 2)


def _s9b_ln3(bx, block):
    nc, g = bx.nc, bx.g
    ccg = g["ccg"]
    NT = 16
    xa = bx.sb("s9b_xa", [128, H], F32)
    xc = bx.sb("s9b_xc", [128, H], F32)
    sqb = bx.sb("s9b_sq", [128, H], F32)
    x3t = bx.sb("s9b_x3", [128, H], F32)
    colsb = bx.sb("s9b_cols", [128, 8], F32)
    rstdb = bx.sb("s9b_rstd", [128, 1], F32)
    epsb = bx.sb("s9b_eps", [128, 1], F32)
    gcol = bx.sb("s9b_gc", [128, 1], F32)
    ld = bx.sem("ld")
    vv = bx.sem("v")
    ss = bx.sem("s")
    st = bx.sem("st")
    VP1 = 9

    def b1v(j):
        return 2 + VP1 * j

    V1 = 2 + VP1 * NT

    @block.sync
    def _(e):
        for j in range(NT):
            if j > 0:
                e.wait_ge(vv, b1v(j - 1) + VP1)
            e.dma_start(out=xa[:, :],
                        in_=g["x2_d"][j * 128:(j + 1) * 128, :]).then_inc(ld, 16)

    @block.vector
    def _(e):
        e.memset(epsb[:, :], 1e-5).then_inc(vv, 1)
        e.memset(gcol[:, :], 0.0).then_inc(vv, 1)
        for j in range(NT):
            b0 = b1v(j)
            e.wait_ge(ld, 16 * (j + 1))
            e.tensor_reduce(colsb[:, 0:1],
                            xa[:, :].rearrange("p (g c) -> p g c", g=1),
                            axis=AX, op=ALU.add).then_inc(vv, 1)
            e.wait_ge(vv, b0 + 1)
            e.tensor_scalar(colsb[:, 1:2], colsb[:, 0:1], -1.0 / H, None,
                            ALU.mult).then_inc(vv, 1)
            e.wait_ge(vv, b0 + 2)
            e.tensor_scalar(xc[:, :], xa[:, :], colsb[:, 1:2], None,
                            ALU.add).then_inc(vv, 1)
            e.wait_ge(vv, b0 + 3)
            e.tensor_tensor(sqb[:, :], xc[:, :], xc[:, :], ALU.mult).then_inc(vv, 1)
            e.wait_ge(vv, b0 + 4)
            e.tensor_reduce(colsb[:, 2:3],
                            sqb[:, :].rearrange("p (g c) -> p g c", g=1),
                            axis=AX, op=ALU.add).then_inc(vv, 1)
            e.wait_ge(ss, j + 1)
            e.reciprocal(colsb[:, 3:4], rstdb[:, 0:1]).then_inc(vv, 1)
            e.wait_ge(vv, b0 + 6)
            if j > 0:
                e.wait_ge(st, 16 * j)
            e.tensor_scalar(x3t[:, :], xc[:, :], colsb[:, 3:4], None,
                            ALU.mult).then_inc(vv, 1)
            e.wait_ge(vv, b0 + 7)
            e.tensor_reduce(colsb[:, 4:5],
                            x3t[:, :].rearrange("p (g c) -> p g c", g=1),
                            axis=AX, op=ALU.max,
                            apply_absolute_value=True).then_inc(vv, 1)
            e.wait_ge(vv, b0 + 8)
            e.tensor_tensor(gcol[:, :], colsb[:, 4:5], gcol[:, :],
                            ALU.max).then_inc(vv, 1)

    @block.scalar
    def _(e):
        for j in range(NT):
            e.wait_ge(vv, b1v(j) + 5)
            e.activation(rstdb[:, 0:1], colsb[:, 2:3], AF.Sqrt,
                         bias=epsb[:, 0:1], scale=1.0 / H).then_inc(ss, 1)

    @block.gpsimd
    def _(e):
        for j in range(NT):
            e.wait_ge(vv, b1v(j) + 7)
            e.dma_start(out=g["x3_d"][j * 128:(j + 1) * 128, :],
                        in_=x3t[:, :]).then_inc(st, 16)
        e.wait_ge(vv, V1)
        e.dma_start(out=g["mxi0"], in_=gcol[:, :]).then_inc(st, 16)
        e.wait_ge(st, 16 * (NT + 1))
        e.collective_compute("AllReduce", ALU.max, replica_groups=G8,
                             ins=[g["mxi0"].opt()],
                             outs=[g["mxo0"].opt()]).then_inc(ccg, 1)
        e.wait_ge(ccg, 3)


def _s9e_upg(bx, block):
    nc, g = bx.nc, bx.g
    PS = bx.PS
    ccg = g["ccg"]
    NT = 16
    m0col = bx.sb("s9e_m0c", [128, 1], F32)
    m0row = bx.sb("s9e_m0r", [1, 128], F32)
    m0loc = bx.sb("s9e_m0l", [1, 1], F32)
    ones1f = bx.sb("s9e_o1", [1, 128], F32)
    s0row = bx.sb("s9e_s0", [1, 1], F32)
    r0row = bx.sb("s9e_r0", [1, 1], F32)
    sc01 = bx.sb("s9e_sc", [128, 2], F32)
    wu_sb = bx.sb("s9e_wu", [128, 8 * 3072])
    wg_sb = bx.sb("s9e_wg", [128, 8 * 3072])
    x3t = bx.sb("s9e_x3", [128, H], F32)
    xq = bx.sb("s9e_xq", [128, H], F32)
    xqbf = bx.sb("s9e_xqb", [128, H], F32)
    xqT = bx.sb("s9e_xqT", [128, 8 * 128])
    up_t = bx.sb("s9e_up", [128, 3072], F32)
    gt_t = bx.sb("s9e_gt", [128, 3072], F32)
    c1 = bx.sb("s9e_c1", [128, 2], F32)
    gm12 = bx.sb("s9e_gm", [128, 2], F32)
    ld = bx.sem("ld")
    tt = bx.sem("t")
    vv = bx.sem("v")
    st = bx.sem("st")
    VPRE = 8
    VP2 = 22
    TP2 = 104
    LDW = 16 + 256

    def b2v(j):
        return VPRE + VP2 * j

    def b2t(j):
        return 3 + TP2 * j

    @block.sync
    def _(e):
        e.dma_start(out=m0col[:, :], in_=g["mxo0"]).then_inc(ld, 16)
        e.wait_ge(ld, 16)
        for k in range(8):
            e.dma_start(out=wu_sb[:, k * 3072:(k + 1) * 3072],
                        in_=g["wugT_all"][k * 128:(k + 1) * 128, 0:3072]).then_inc(ld, 16)
        for k in range(8):
            e.dma_start(out=wg_sb[:, k * 3072:(k + 1) * 3072],
                        in_=g["wugT_all"][k * 128:(k + 1) * 128, 3072:6144]).then_inc(ld, 16)
        e.wait_ge(ld, LDW)
        for j in range(NT):
            if j > 0:
                e.wait_ge(vv, b2v(j - 1) + VP2)
            e.dma_start(out=x3t[:, :],
                        in_=g["x3_d"][j * 128:(j + 1) * 128, :]).then_inc(ld, 16)

    @block.tensor
    def _(e):
        e.wait_ge(ld, 16)
        e.transpose(PS[2][:1, 0:128], m0col[:, :], g["identf_s"][:, :]).then_inc(tt, 1)
        e.wait_ge(vv, 6)
        e.matmul(PS[2][:, 0:1], ones1f[:1, :], s0row[:1, :],
                 start=True, stop=True).then_inc(tt, 1)
        e.matmul(PS[2][:, 1:2], ones1f[:1, :], r0row[:1, :],
                 start=True, stop=True).then_inc(tt, 1)
        e.wait_ge(ld, LDW)
        for j in range(NT):
            e.wait_ge(ld, LDW + 16 * (j + 1))
            e.wait_ge(vv, b2v(j) + 4)
            for c in range(8):
                bk = 6 + c // 4
                if c % 4 == 0 and j > 0:
                    e.wait_ge(vv, b2v(j - 1) + 5 + c // 4 + 1)
                e.transpose(PS[bk][:, (c % 4) * 128:(c % 4) * 128 + 128],
                            xqbf[:, c * 128:(c + 1) * 128],
                            g["identf_s"][:, :]).then_inc(tt, 1)
            e.wait_ge(vv, b2v(j) + 6)
            if j > 0:
                e.wait_ge(vv, b2v(j - 1) + 20)
            for n in range(6):
                for k in range(8):
                    e.matmul(PS[n][:, :], xqT[:, k * 128:(k + 1) * 128],
                             wu_sb[:, k * 3072 + n * 512:k * 3072 + n * 512 + 512],
                             start=(k == 0), stop=(k == 7)).then_inc(tt, 1)
            e.wait_ge(vv, b2v(j) + 12)
            for n in range(6):
                for k in range(8):
                    e.matmul(PS[n][:, :], xqT[:, k * 128:(k + 1) * 128],
                             wg_sb[:, k * 3072 + n * 512:k * 3072 + n * 512 + 512],
                             start=(k == 0), stop=(k == 7)).then_inc(tt, 1)

    @block.vector
    def _(e):
        e.memset(ones1f[:, :], 1.0).then_inc(vv, 1)
        e.memset(gm12[:, :], 0.0).then_inc(vv, 1)
        e.wait_ge(tt, 1)
        e.tensor_copy(m0row[:1, :], PS[2][:1, 0:128]).then_inc(vv, 1)
        e.wait_ge(vv, 3)
        e.tensor_reduce(m0loc[:1, :],
                        m0row[:1, :].rearrange("p (g c) -> p g c", g=1),
                        axis=AX, op=ALU.max).then_inc(vv, 1)
        e.wait_ge(vv, 4)
        e.tensor_scalar(s0row[:, :], m0loc[:1, 0:1], 1.0 / QMAX, 1e-12,
                        ALU.mult, ALU.max).then_inc(vv, 1)
        e.wait_ge(vv, 5)
        e.reciprocal(r0row[:, :], s0row[:, :]).then_inc(vv, 1)
        e.wait_ge(tt, 3)
        e.tensor_copy(sc01[:, 0:2], PS[2][:, 0:2]).then_inc(vv, 1)
        e.nop().then_inc(vv, 1)
        for j in range(NT):
            b0 = b2v(j)
            e.wait_ge(ld, LDW + 16 * (j + 1))
            e.tensor_scalar(xq[:, :], x3t[:, :], sc01[:, 1:2], None,
                            ALU.mult).then_inc(vv, 1)
            e.wait_ge(vv, b0 + 1)
            e.tensor_scalar(xq[:, :], xq[:, :], MAGIC, -MAGIC,
                            ALU.add, ALU.add).then_inc(vv, 1)
            e.wait_ge(vv, b0 + 2)
            e.tensor_scalar(xq[:, :], xq[:, :], QMAX, -QMAX - 1.0,
                            ALU.min, ALU.max).then_inc(vv, 1)
            e.wait_ge(vv, b0 + 3)
            if j > 0:
                e.wait_ge(tt, b2t(j - 1) + 8)
            e.tensor_scalar(xqbf[:, :], xq[:, :], sc01[:, 0:1], None,
                            ALU.mult).then_inc(vv, 1)
            e.wait_ge(tt, b2t(j) + 4)
            e.tensor_copy(xqT[:, 0:512], PS[6][:, 0:512]).then_inc(vv, 1)
            e.wait_ge(tt, b2t(j) + 8)
            e.tensor_copy(xqT[:, 512:1024], PS[7][:, 0:512]).then_inc(vv, 1)
            for n in range(6):
                e.wait_ge(tt, b2t(j) + 8 + 8 * (n + 1))
                if n == 0 and j > 0:
                    e.wait_ge(st, 32 * j)
                e.tensor_copy(up_t[:, n * 512:(n + 1) * 512],
                              PS[n][:, :]).then_inc(vv, 1)
            e.wait_ge(vv, b0 + 12)
            e.tensor_reduce(c1[:, 0:1],
                            up_t[:, :].rearrange("p (g c) -> p g c", g=1),
                            axis=AX, op=ALU.max,
                            apply_absolute_value=True).then_inc(vv, 1)
            e.wait_ge(vv, b0 + 13)
            e.tensor_tensor(gm12[:, 0:1], c1[:, 0:1], gm12[:, 0:1],
                            ALU.max).then_inc(vv, 1)
            for n in range(6):
                e.wait_ge(tt, b2t(j) + 56 + 8 * (n + 1))
                e.tensor_copy(gt_t[:, n * 512:(n + 1) * 512],
                              PS[n][:, :]).then_inc(vv, 1)
            e.wait_ge(vv, b0 + 20)
            e.tensor_reduce(c1[:, 1:2],
                            gt_t[:, :].rearrange("p (g c) -> p g c", g=1),
                            axis=AX, op=ALU.max,
                            apply_absolute_value=True).then_inc(vv, 1)
            e.wait_ge(vv, b0 + 21)
            e.tensor_tensor(gm12[:, 1:2], c1[:, 1:2], gm12[:, 1:2],
                            ALU.max).then_inc(vv, 1)

    @block.gpsimd
    def _(e):
        for j in range(NT):
            e.wait_ge(vv, b2v(j) + 12)
            e.dma_start(out=g["upraw_d"][j * 128:(j + 1) * 128, :],
                        in_=up_t[:, :]).then_inc(st, 16)
            e.wait_ge(vv, b2v(j) + 20)
            e.dma_start(out=g["gateraw_d"][j * 128:(j + 1) * 128, :],
                        in_=gt_t[:, :]).then_inc(st, 16)
        e.wait_ge(vv, b2v(NT - 1) + VP2)
        e.dma_start(out=g["mxi1"], in_=gm12[:, :]).then_inc(st, 16)
        e.wait_ge(st, 32 * NT + 16)
        e.collective_compute("AllReduce", ALU.max, replica_groups=G8,
                             ins=[g["mxi1"].opt()],
                             outs=[g["mxo1"].opt()]).then_inc(ccg, 1)
        e.wait_ge(ccg, 4)


def _s9f_sig(bx, block):
    nc, g = bx.nc, bx.g
    PS = bx.PS
    ccg = g["ccg"]
    NT = 16
    m12col = bx.sb("s9f_m12", [128, 2], F32)
    m2row = bx.sb("s9f_m2r", [1, 128], F32)
    m2loc = bx.sb("s9f_m2l", [1, 1], F32)
    ones1f = bx.sb("s9f_o1", [1, 128], F32)
    sgrow = bx.sb("s9f_sg", [1, 2], F32)
    sc2f = bx.sb("s9f_sc", [128, 2], F32)
    gr_t = bx.sb("s9f_gr", [128, 3072], F32)
    gr_t2 = bx.sb("s9f_gr2", [128, 3072], F32)
    gq = bx.sb("s9f_gq", [128, 3072], F32)
    sg = bx.sb("s9f_sig", [128, 3072], F32)
    g2 = bx.sb("s9f_g2", [128, 3072], F32)
    c3 = bx.sb("s9f_c3", [128, 1], F32)
    gm3 = bx.sb("s9f_gm3", [128, 1], F32)
    ld = bx.sem("ld")
    l1 = bx.sem("l1")
    tt = bx.sem("t")
    vv = bx.sem("v")
    ss = bx.sem("s")
    st = bx.sem("st")
    VPRE = 8
    VPB = 11

    def bBv(j):
        return VPRE + VPB * j

    @block.sync
    def _(e):
        e.dma_start(out=m12col[:, :], in_=g["mxo1"]).then_inc(ld, 16)
        e.wait_ge(ld, 16)
        for j in range(NT):
            if j > 1:
                e.wait_ge(vv, bBv(j - 2) + 1)
            e.dma_start(out=(gr_t if j % 2 == 0 else gr_t2)[:, :],
                        in_=g["gateraw_d"][j * 128:(j + 1) * 128, :]
                        ).then_inc(ld if j % 2 == 0 else l1, 16)

    @block.tensor
    def _(e):
        e.wait_ge(ld, 16)
        e.transpose(PS[2][:1, 0:128], m12col[:, 1:2], g["identf_s"][:, :]).then_inc(tt, 1)
        e.wait_ge(vv, 6)
        e.matmul(PS[2][:, 0:1], ones1f[:1, :], sgrow[:1, 0:1],
                 start=True, stop=True).then_inc(tt, 1)
        e.matmul(PS[2][:, 1:2], ones1f[:1, :], sgrow[:1, 1:2],
                 start=True, stop=True).then_inc(tt, 1)

    @block.vector
    def _(e):
        e.memset(ones1f[:, :], 1.0).then_inc(vv, 1)
        e.memset(gm3[:, :], 0.0).then_inc(vv, 1)
        e.wait_ge(tt, 1)
        e.tensor_copy(m2row[:1, :], PS[2][:1, 0:128]).then_inc(vv, 1)
        e.wait_ge(vv, 3)
        e.tensor_reduce(m2loc[:1, :],
                        m2row[:1, :].rearrange("p (g c) -> p g c", g=1),
                        axis=AX, op=ALU.max).then_inc(vv, 1)
        e.wait_ge(vv, 4)
        e.tensor_scalar(sgrow[:1, 0:1], m2loc[:1, 0:1], 1.0 / QMAX, 1e-12,
                        ALU.mult, ALU.max).then_inc(vv, 1)
        e.wait_ge(vv, 5)
        e.reciprocal(sgrow[:1, 1:2], sgrow[:1, 0:1]).then_inc(vv, 1)
        e.wait_ge(tt, 3)
        e.tensor_copy(sc2f[:, 0:2], PS[2][:, 0:2]).then_inc(vv, 1)
        e.nop().then_inc(vv, 1)
        for j in range(NT):
            b0 = bBv(j)
            if j % 2 == 0:
                e.wait_ge(ld, 16 + 16 * (j // 2 + 1))
            else:
                e.wait_ge(l1, 16 * (j // 2 + 1))
            e.tensor_scalar(gq[:, :], (gr_t if j % 2 == 0 else gr_t2)[:, :],
                            sc2f[:, 1:2], None,
                            ALU.mult).then_inc(vv, 1)
            e.wait_ge(vv, b0 + 1)
            e.tensor_scalar(gq[:, :], gq[:, :], MAGIC, -MAGIC,
                            ALU.add, ALU.add).then_inc(vv, 1)
            e.wait_ge(vv, b0 + 2)
            e.tensor_scalar(gq[:, :], gq[:, :], QMAX, -QMAX - 1.0,
                            ALU.min, ALU.max).then_inc(vv, 1)
            e.wait_ge(vv, b0 + 3)
            e.tensor_scalar(gq[:, :], gq[:, :], sc2f[:, 0:1], None,
                            ALU.mult).then_inc(vv, 1)
            e.wait_ge(ss, j + 1)
            e.tensor_scalar(sg[:, :], sg[:, :], 65536.0, None,
                            ALU.mult).then_inc(vv, 1)
            e.wait_ge(vv, b0 + 5)
            e.tensor_scalar(sg[:, :], sg[:, :], MAGIC, -MAGIC,
                            ALU.add, ALU.add).then_inc(vv, 1)
            e.wait_ge(vv, b0 + 6)
            e.tensor_scalar(sg[:, :], sg[:, :], 65535.0, 0.0,
                            ALU.min, ALU.max).then_inc(vv, 1)
            e.wait_ge(vv, b0 + 7)
            e.tensor_scalar(sg[:, :], sg[:, :], 1.0 / 65536.0, None,
                            ALU.mult).then_inc(vv, 1)
            e.wait_ge(vv, b0 + 8)
            if j > 0:
                e.wait_ge(st, 16 * j)
            e.tensor_tensor(g2[:, :], gq[:, :], sg[:, :], ALU.mult).then_inc(vv, 1)
            e.wait_ge(vv, b0 + 9)
            e.tensor_reduce(c3[:, 0:1],
                            g2[:, :].rearrange("p (g c) -> p g c", g=1),
                            axis=AX, op=ALU.max,
                            apply_absolute_value=True).then_inc(vv, 1)
            e.wait_ge(vv, b0 + 10)
            e.tensor_tensor(gm3[:, :], c3[:, 0:1], gm3[:, :],
                            ALU.max).then_inc(vv, 1)

    @block.scalar
    def _(e):
        for j in range(NT):
            e.wait_ge(vv, bBv(j) + 4)
            e.activation(sg[:, :], gq[:, :], AF.Sigmoid).then_inc(ss, 1)

    @block.gpsimd
    def _(e):
        for j in range(NT):
            e.wait_ge(vv, bBv(j) + 9)
            e.dma_start(out=g["g2raw_d"][j * 128:(j + 1) * 128, :],
                        in_=g2[:, :]).then_inc(st, 16)
        e.wait_ge(vv, bBv(NT - 1) + VPB)
        e.dma_start(out=g["mxi3"], in_=gm3[:, :]).then_inc(st, 16)
        e.wait_ge(st, 16 * NT + 16)
        e.collective_compute("AllReduce", ALU.max, replica_groups=G8,
                             ins=[g["mxi3"].opt()],
                             outs=[g["mxo3"].opt()]).then_inc(ccg, 1)
        e.wait_ge(ccg, 5)


def _s9g_oraw(bx, block):
    nc, g = bx.nc, bx.g
    PS = bx.PS
    ccg = g["ccg"]
    NT = 16
    m12col = bx.sb("s9g_m12", [128, 2], F32)
    m3col = bx.sb("s9g_m3c", [128, 1], F32)
    mrow = bx.sb("s9g_mr", [1, 256], F32)
    mloc = bx.sb("s9g_ml", [1, 2], F32)
    ones1f = bx.sb("s9g_o1", [1, 128], F32)
    srowg = bx.sb("s9g_sr", [1, 4], F32)
    sc4 = bx.sb("s9g_sc", [128, 4], F32)
    gr_t = bx.sb("s9g_gr", [128, 3072], F32)
    gr_t2 = bx.sb("s9g_gr2", [128, 3072], F32)
    up_t = bx.sb("s9g_up", [128, 3072], F32)
    up_t2 = bx.sb("s9g_up2", [128, 3072], F32)
    gq = bx.sb("s9g_gq", [128, 3072], F32)
    uq = bx.sb("s9g_uq", [128, 3072], F32)
    g2 = bx.sb("s9g_g2", [128, 3072], F32)
    c4 = bx.sb("s9g_c4", [128, 1], F32)
    gm4 = bx.sb("s9g_gm4", [128, 1], F32)
    ld = bx.sem("ld")
    l1 = bx.sem("l1")
    tt = bx.sem("t")
    vv = bx.sem("v")
    st = bx.sem("st")
    VPRE = 10
    VPC = 11

    def bCv(j):
        return VPRE + VPC * j

    @block.sync
    def _(e):
        e.dma_start(out=m12col[:, :], in_=g["mxo1"]).then_inc(ld, 16)
        e.dma_start(out=m3col[:, :], in_=g["mxo3"]).then_inc(ld, 16)
        e.wait_ge(ld, 32)
        for j in range(NT):
            if j > 1:
                e.wait_ge(vv, bCv(j - 2) + 5)
            sp = ld if j % 2 == 0 else l1
            e.dma_start(out=(gr_t if j % 2 == 0 else gr_t2)[:, :],
                        in_=g["g2raw_d"][j * 128:(j + 1) * 128, :]).then_inc(sp, 16)
            e.dma_start(out=(up_t if j % 2 == 0 else up_t2)[:, :],
                        in_=g["upraw_d"][j * 128:(j + 1) * 128, :]).then_inc(sp, 16)

    @block.tensor
    def _(e):
        e.wait_ge(ld, 32)
        e.transpose(PS[2][:1, 0:128], m12col[:, 0:1], g["identf_s"][:, :]).then_inc(tt, 1)
        e.transpose(PS[2][:1, 128:256], m3col[:, 0:1], g["identf_s"][:, :]).then_inc(tt, 1)
        e.wait_ge(vv, 8)
        for c in range(4):
            e.matmul(PS[2][:, c:c + 1], ones1f[:1, :], srowg[:1, c:c + 1],
                     start=True, stop=True).then_inc(tt, 1)

    @block.vector
    def _(e):
        e.memset(ones1f[:, :], 1.0).then_inc(vv, 1)
        e.memset(gm4[:, :], 0.0).then_inc(vv, 1)
        e.wait_ge(tt, 2)
        e.tensor_copy(mrow[:1, :], PS[2][:1, 0:256]).then_inc(vv, 1)
        e.wait_ge(vv, 3)
        e.tensor_reduce(mloc[:1, :],
                        mrow[:1, :].rearrange("p (g c) -> p g c", g=2),
                        axis=AX, op=ALU.max).then_inc(vv, 1)
        e.wait_ge(vv, 4)
        e.tensor_scalar(srowg[:1, 0:1], mloc[:1, 0:1], 1.0 / QMAX, 1e-12,
                        ALU.mult, ALU.max).then_inc(vv, 1)
        e.wait_ge(vv, 5)
        e.reciprocal(srowg[:1, 1:2], srowg[:1, 0:1]).then_inc(vv, 1)
        e.tensor_scalar(srowg[:1, 2:3], mloc[:1, 1:2], 1.0 / QMAX, 1e-12,
                        ALU.mult, ALU.max).then_inc(vv, 1)
        e.wait_ge(vv, 7)
        e.reciprocal(srowg[:1, 3:4], srowg[:1, 2:3]).then_inc(vv, 1)
        e.wait_ge(tt, 6)
        e.tensor_copy(sc4[:, 0:4], PS[2][:, 0:4]).then_inc(vv, 1)
        e.nop().then_inc(vv, 1)
        for j in range(NT):
            b0 = bCv(j)
            if j % 2 == 0:
                e.wait_ge(ld, 32 + 32 * (j // 2 + 1))
            else:
                e.wait_ge(l1, 32 * (j // 2 + 1))
            e.tensor_scalar(gq[:, :], (gr_t if j % 2 == 0 else gr_t2)[:, :],
                            sc4[:, 3:4], None,
                            ALU.mult).then_inc(vv, 1)
            e.wait_ge(vv, b0 + 1)
            e.tensor_scalar(gq[:, :], gq[:, :], MAGIC, -MAGIC,
                            ALU.add, ALU.add).then_inc(vv, 1)
            e.wait_ge(vv, b0 + 2)
            e.tensor_scalar(gq[:, :], gq[:, :], QMAX, -QMAX - 1.0,
                            ALU.min, ALU.max).then_inc(vv, 1)
            e.wait_ge(vv, b0 + 3)
            e.tensor_scalar(gq[:, :], gq[:, :], sc4[:, 2:3], None,
                            ALU.mult).then_inc(vv, 1)
            e.tensor_scalar(uq[:, :], (up_t if j % 2 == 0 else up_t2)[:, :],
                            sc4[:, 1:2], None,
                            ALU.mult).then_inc(vv, 1)
            e.wait_ge(vv, b0 + 5)
            e.tensor_scalar(uq[:, :], uq[:, :], MAGIC, -MAGIC,
                            ALU.add, ALU.add).then_inc(vv, 1)
            e.wait_ge(vv, b0 + 6)
            e.tensor_scalar(uq[:, :], uq[:, :], QMAX, -QMAX - 1.0,
                            ALU.min, ALU.max).then_inc(vv, 1)
            e.wait_ge(vv, b0 + 7)
            e.tensor_scalar(uq[:, :], uq[:, :], sc4[:, 0:1], None,
                            ALU.mult).then_inc(vv, 1)
            e.wait_ge(vv, b0 + 8)
            if j > 0:
                e.wait_ge(st, 16 * j)
            e.tensor_tensor(g2[:, :], gq[:, :], uq[:, :], ALU.mult).then_inc(vv, 1)
            e.wait_ge(vv, b0 + 9)
            e.tensor_reduce(c4[:, 0:1],
                            g2[:, :].rearrange("p (g c) -> p g c", g=1),
                            axis=AX, op=ALU.max,
                            apply_absolute_value=True).then_inc(vv, 1)
            e.wait_ge(vv, b0 + 10)
            e.tensor_tensor(gm4[:, :], c4[:, 0:1], gm4[:, :],
                            ALU.max).then_inc(vv, 1)

    @block.gpsimd
    def _(e):
        for j in range(NT):
            e.wait_ge(vv, bCv(j) + 9)
            e.dma_start(out=g["oraw_d"][j * 128:(j + 1) * 128, :],
                        in_=g2[:, :]).then_inc(st, 16)
        e.wait_ge(vv, bCv(NT - 1) + VPC)
        e.dma_start(out=g["mxi4"], in_=gm4[:, :]).then_inc(st, 16)
        e.wait_ge(st, 16 * NT + 16)
        e.collective_compute("AllReduce", ALU.max, replica_groups=G8,
                             ins=[g["mxi4"].opt()],
                             outs=[g["mxo4"].opt()]).then_inc(ccg, 1)
        e.wait_ge(ccg, 6)


def _s9_down(bx, block):
    nc, g = bx.nc, bx.g
    PS = bx.PS
    NT = 16
    wd_sb = bx.sb("s9d_wd", [128, 24 * 1024])
    m4col = bx.sb("s9d_m4c", [128, 1], F32)
    m4row = bx.sb("s9d_m4r", [1, 128], F32)
    m4loc = bx.sb("s9d_m4l", [1, 1], F32)
    ones1f = bx.sb("s9d_o1", [1, 128], F32)
    s4row = bx.sb("s9d_s4", [1, 1], F32)
    r4row = bx.sb("s9d_r4", [1, 1], F32)
    sc2 = bx.sb("s9d_sc2", [128, 2], F32)
    or_t = bx.sb("s9d_or", [128, 3072], F32)
    oq = bx.sb("s9d_oq", [128, 3072], F32)
    obf = bx.sb("s9d_ob", [128, 3072], F32)
    oT = bx.sb("s9d_oT", [128, 24 * 128])
    ff_t = bx.sb("s9d_ff", [128, H], F32)
    x2b = bx.sb("s9d_x2", [128, H], F32)
    xc = bx.sb("s9d_xc", [128, H], F32)
    sqb = bx.sb("s9d_sq", [128, H], F32)
    lnt = bx.sb("s9d_ln", [128, H], F32)
    outt = bx.sb("s9d_out", [128, H], F32)
    colsb = bx.sb("s9d_cols", [128, 8], F32)
    rstdb = bx.sb("s9d_rstd", [128, 1], F32)
    epsb = bx.sb("s9d_eps", [128, 1], F32)
    ld = bx.sem("ld")
    tt = bx.sem("t")
    vv = bx.sem("v")
    ss = bx.sem("s")
    st = bx.sem("st")
    VPRE = 8
    VPD = 20
    TPD = 72
    LDW = 16 + 24 * 16

    def bDv(j):
        return VPRE + VPD * j

    def bDt(j):
        return 3 + TPD * j

    @block.sync
    def _(e):
        e.dma_start(out=m4col[:, :], in_=g["mxo4"]).then_inc(ld, 16)
        e.wait_ge(ld, 16)
        for k in range(24):
            e.dma_start(out=wd_sb[:, k * 1024:(k + 1) * 1024],
                        in_=g["wdT_all"][k * 128:(k + 1) * 128, :]).then_inc(ld, 16)
        e.wait_ge(ld, LDW)
        for j in range(NT):
            if j > 0:
                e.wait_ge(vv, bDv(j - 1) + VPD)
            e.dma_start(out=or_t[:, :],
                        in_=g["oraw_d"][j * 128:(j + 1) * 128, :]).then_inc(ld, 16)
            e.dma_start(out=x2b[:, :],
                        in_=g["x2_d"][j * 128:(j + 1) * 128, :]).then_inc(ld, 16)

    @block.tensor
    def _(e):
        e.wait_ge(ld, 16)
        e.transpose(PS[2][:1, 0:128], m4col[:, :], g["identf_s"][:, :]).then_inc(tt, 1)
        e.wait_ge(vv, 6)
        e.matmul(PS[2][:, 0:1], ones1f[:1, :], s4row[:1, :],
                 start=True, stop=True).then_inc(tt, 1)
        e.matmul(PS[2][:, 1:2], ones1f[:1, :], r4row[:1, :],
                 start=True, stop=True).then_inc(tt, 1)
        e.wait_ge(ld, LDW)
        for j in range(NT):
            e.wait_ge(ld, LDW + 32 * (j + 1))
            e.wait_ge(vv, bDv(j) + 4)
            for t in range(6):
                bk = 6 + t % 2
                if t >= 2:
                    e.wait_ge(vv, bDv(j) + 4 + (t - 2) + 1)
                elif j > 0:
                    e.wait_ge(vv, bDv(j - 1) + 4 + (4 + t) + 1)
                for c in range(4):
                    e.transpose(PS[bk][:, c * 128:(c + 1) * 128],
                                obf[:, (t * 4 + c) * 128:(t * 4 + c) * 128 + 128],
                                g["identf_s"][:, :]).then_inc(tt, 1)
            e.wait_ge(vv, bDv(j) + 10)
            if j > 0:
                e.wait_ge(vv, bDv(j - 1) + 12)
            for n in range(2):
                for k in range(24):
                    e.matmul(PS[n][:, :], oT[:, k * 128:(k + 1) * 128],
                             wd_sb[:, k * 1024 + n * 512:k * 1024 + n * 512 + 512],
                             start=(k == 0), stop=(k == 23)).then_inc(tt, 1)

    @block.vector
    def _(e):
        e.memset(epsb[:, :], 1e-5).then_inc(vv, 1)
        e.memset(ones1f[:, :], 1.0).then_inc(vv, 1)
        e.wait_ge(tt, 1)
        e.tensor_copy(m4row[:1, :], PS[2][:1, 0:128]).then_inc(vv, 1)
        e.wait_ge(vv, 3)
        e.tensor_reduce(m4loc[:1, :],
                        m4row[:1, :].rearrange("p (g c) -> p g c", g=1),
                        axis=AX, op=ALU.max).then_inc(vv, 1)
        e.wait_ge(vv, 4)
        e.tensor_scalar(s4row[:, :], m4loc[:1, 0:1], 1.0 / QMAX, 1e-12,
                        ALU.mult, ALU.max).then_inc(vv, 1)
        e.wait_ge(vv, 5)
        e.reciprocal(r4row[:, :], s4row[:, :]).then_inc(vv, 1)
        e.wait_ge(tt, 3)
        e.tensor_copy(sc2[:, 0:2], PS[2][:, 0:2]).then_inc(vv, 1)
        e.nop().then_inc(vv, 1)
        for j in range(NT):
            b0 = bDv(j)
            e.wait_ge(ld, LDW + 32 * (j + 1))
            e.tensor_scalar(oq[:, :], or_t[:, :], sc2[:, 1:2], None,
                            ALU.mult).then_inc(vv, 1)
            e.wait_ge(vv, b0 + 1)
            e.tensor_scalar(oq[:, :], oq[:, :], MAGIC, -MAGIC,
                            ALU.add, ALU.add).then_inc(vv, 1)
            e.wait_ge(vv, b0 + 2)
            e.tensor_scalar(oq[:, :], oq[:, :], QMAX, -QMAX - 1.0,
                            ALU.min, ALU.max).then_inc(vv, 1)
            e.wait_ge(vv, b0 + 3)
            if j > 0:
                e.wait_ge(tt, bDt(j - 1) + 24)
            e.tensor_scalar(obf[:, :], oq[:, :], sc2[:, 0:1], None,
                            ALU.mult).then_inc(vv, 1)
            for t in range(6):
                e.wait_ge(tt, bDt(j) + 4 * (t + 1))
                e.tensor_copy(oT[:, t * 512:(t + 1) * 512],
                              PS[6 + t % 2][:, 0:512]).then_inc(vv, 1)
            e.wait_ge(tt, bDt(j) + 24 + 24)
            e.tensor_copy(ff_t[:, 0:512], PS[0][:, :]).then_inc(vv, 1)
            e.wait_ge(tt, bDt(j) + 24 + 48)
            e.tensor_copy(ff_t[:, 512:1024], PS[1][:, :]).then_inc(vv, 1)
            e.wait_ge(vv, b0 + 12)
            e.tensor_reduce(colsb[:, 0:1],
                            ff_t[:, :].rearrange("p (g c) -> p g c", g=1),
                            axis=AX, op=ALU.add).then_inc(vv, 1)
            e.wait_ge(vv, b0 + 13)
            e.tensor_scalar(colsb[:, 1:2], colsb[:, 0:1], -1.0 / H, None,
                            ALU.mult).then_inc(vv, 1)
            e.wait_ge(vv, b0 + 14)
            e.tensor_scalar(xc[:, :], ff_t[:, :], colsb[:, 1:2], None,
                            ALU.add).then_inc(vv, 1)
            e.wait_ge(vv, b0 + 15)
            e.tensor_tensor(sqb[:, :], xc[:, :], xc[:, :], ALU.mult).then_inc(vv, 1)
            e.wait_ge(vv, b0 + 16)
            e.tensor_reduce(colsb[:, 2:3],
                            sqb[:, :].rearrange("p (g c) -> p g c", g=1),
                            axis=AX, op=ALU.add).then_inc(vv, 1)
            e.wait_ge(ss, j + 1)
            e.reciprocal(colsb[:, 3:4], rstdb[:, 0:1]).then_inc(vv, 1)
            e.wait_ge(vv, b0 + 18)
            e.tensor_scalar(lnt[:, :], xc[:, :], colsb[:, 3:4], None,
                            ALU.mult).then_inc(vv, 1)
            e.wait_ge(vv, b0 + 19)
            if j > 0:
                e.wait_ge(st, 16 * j)
            e.tensor_tensor(outt[:, :], lnt[:, :], x2b[:, :],
                            ALU.add).then_inc(vv, 1)

    @block.scalar
    def _(e):
        for j in range(NT):
            e.wait_ge(vv, bDv(j) + 17)
            e.activation(rstdb[:, 0:1], colsb[:, 2:3], AF.Sqrt,
                         bias=epsb[:, 0:1], scale=1.0 / H).then_inc(ss, 1)

    @block.gpsimd
    def _(e):
        for j in range(NT):
            e.wait_ge(vv, bDv(j) + 20)
            e.dma_start(out=g["out"][j * 128:(j + 1) * 128, :],
                        in_=outt[:, :]).then_inc(st, 16)
        e.wait_ge(st, 16 * NT)


def make_in_maps(inputs):
    xs = np.ascontiguousarray(np.asarray(inputs["L0_attn_hidden_states"], np.float32))
    wq = np.asarray(inputs["L0_attn_Wq"], np.float32)
    wk = np.asarray(inputs["L0_attn_Wk"], np.float32)
    wv = np.asarray(inputs["L0_attn_Wv"], np.float32)
    wb = np.asarray(inputs["L0_attn_Wb"], np.float32)
    cq = np.asarray(inputs["L0_attn_conv_q_w"], np.float32)
    ck = np.asarray(inputs["L0_attn_conv_k_w"], np.float32)
    cv = np.asarray(inputs["L0_attn_conv_v_w"], np.float32)
    fsw = np.asarray(inputs["L0_attn_fir_short_w"], np.float32).reshape(1024, 3)
    flw = np.asarray(inputs["L0_attn_fir_long_w"], np.float32).reshape(1024, 63)
    flw64 = np.zeros((1024, 64), np.float32)
    flw64[:, :63] = flw
    w1 = np.ascontiguousarray(np.asarray(inputs["L0_attn_mlp_w1"], np.float32))
    b1 = np.asarray(inputs["L0_attn_mlp_b1"], np.float32).reshape(1, 2048)
    w2 = np.ascontiguousarray(np.asarray(inputs["L0_attn_mlp_w2"], np.float32))
    b2 = np.asarray(inputs["L0_attn_mlp_b2"], np.float32).reshape(1, 16)
    glt = np.asarray(inputs["L0_attn_gate_log_temp"], np.float32).reshape(1, 4)
    onw = np.asarray(inputs["L0_attn_o_norm_w"], np.float32).reshape(1, 256)
    wo = np.ascontiguousarray(np.asarray(inputs["L0_attn_Wo"], np.float32))
    wgt = np.asarray(inputs["L0_ffn_w_gate"], np.float32)
    wup = np.asarray(inputs["L0_ffn_w_up"], np.float32)
    wdn = np.asarray(inputs["L0_ffn_w_down"], np.float32)

    in_maps = []
    for c in range(8):
        b, half = c // 2, c % 2
        cols = slice(CH * half, CH * (half + 1))
        m = dict(
            x=np.ascontiguousarray(xs[b]),
            x_my=np.ascontiguousarray(xs[b, half * LH:(half + 1) * LH, :]),
            tok0=np.array([[half * LH]], np.int32),
            wq=np.ascontiguousarray(wq[:, cols]),
            wk=np.ascontiguousarray(wk[:, cols]),
            wv=np.ascontiguousarray(wv[:, cols]),
            wb=np.ascontiguousarray(wb[:, 2 * half:2 * half + 2]),
            cqw=np.ascontiguousarray(cq[cols]),
            ckw=np.ascontiguousarray(ck[cols]),
            cvw=np.ascontiguousarray(cv[cols]),
            fsw=np.ascontiguousarray(fsw[cols]),
            flw=np.ascontiguousarray(flw64[cols]),
            w1=np.ascontiguousarray(w1[:1024]),
            w1s=_w1s_eff(w1, half), b1=b1, w2=w2, b2=b2, glt=glt, onw=onw,
            onwc=np.ascontiguousarray(onw.reshape(256, 1)),
            wsel=_wsel_eff(half),
            mA=np.full((128, 1), 1.0 - half, np.float32),
            mB=np.full((128, 1), float(half), np.float32),
            wo=np.ascontiguousarray(wo[CH * half:CH * (half + 1)]),
            wu_my=np.ascontiguousarray(wup[:, c * 128:(c + 1) * 128]),
            wg_my=np.ascontiguousarray(wgt[:, c * 128:(c + 1) * 128]),
            wd_my=np.ascontiguousarray(wdn[:, c * 384:(c + 1) * 384]),
        )
        in_maps.append(m)
    return in_maps


def _w1s_eff(w1, half):
    out = np.zeros((64, 2048), np.float32)
    for R in range(2):
        for kind in range(2):
            for b_ in range(4):
                for i in range(2):
                    out[R * 32 + half * 16 + kind * 8 + b_ * 2 + i] = \
                        w1[1024 + b_ * 8 + kind * 4 + R * 2 + i]
    return np.ascontiguousarray(out)


def _wsel_eff(half):
    out = np.zeros((16, 8), np.float32)
    for m in range(8):
        out[(2 * half + m // 4) * 4 + m % 4, m] = 1.0
    return out


def kernel(**inputs):
    nc = build_program()
    in_maps = make_in_maps(inputs)
    res = run_bass_kernel_spmd(nc, in_maps, core_ids=list(range(8)))
    outp = np.zeros((4, L, H), np.float32)
    for c in range(8):
        b, half = c // 2, c % 2
        outp[b, half * LH:(half + 1) * LH, :] = np.asarray(res.results[c]["out"])
    return outp
